# version 23 (final)
# baseline (speedup 1.0000x reference)
import numpy as np
import ml_dtypes
from contextlib import ExitStack
import concourse.bass as bass
import concourse.mybir as mybir
from concourse.bass_utils import run_bass_kernel_spmd

F32 = mybir.dt.float32
BF16 = mybir.dt.bfloat16
I32 = mybir.dt.int32
AF = mybir.ActivationFunctionType
ALU = mybir.AluOpType
AX = mybir.AxisListType

EXP_SCALE = float(1.0 / np.sqrt(128.0))
LN_EPS = 1e-5
NEG_BIG = -1.0e30

FULL_CFG = dict(D=2048, T=2048, H=16, E=8, I=2048)
_LAST_P = None

ENGS = ("pe", "act", "dve", "pool", "sp")
ESEM = {"pe": "sPE", "act": "sACT", "dve": "sDVE", "pool": "sPOOL", "sp": "sSP"}
DSEM = {"pool": "dPOOL", "sp": "dSP"}
ALL_SEMS = list(ESEM.values()) + list(DSEM.values())
BIG = 1 << 30


class Prog:

    def __init__(self):
        from collections import defaultdict
        self.q = {e: [] for e in ENGS}
        self.cnt = defaultdict(int)
        self.hw = {e: defaultdict(int) for e in ENGS}
        self.reg = {}
        self.semnames = set(ESEM.values())
        self.psum_gran = {}

    def _wait(self, eng, sem, val):
        if val > self.hw[eng][sem]:
            self.hw[eng][sem] = val
            self.q[eng].append(("w", sem, val))

    def _gran(self, t, lo, hi):
        g = self.psum_gran.get(t)
        if g:
            lo = (lo // g) * g
            hi = ((hi + g - 1) // g) * g
        return lo, hi

    def grp(self, eng, emitters, R=(), W=(), dma=False, dsem=None):
        R = [(t,) + self._gran(t, lo, hi) for (t, lo, hi) in R]
        W = [(t,) + self._gran(t, lo, hi) for (t, lo, hi) in W]
        waits = {}
        for (t, lo, hi) in R:
            rec = self.reg.get(t)
            if rec:
                for (l, h, s, v) in rec["w"]:
                    if l < hi and lo < h and v > waits.get(s, 0):
                        waits[s] = v
        for (t, lo, hi) in W:
            rec = self.reg.get(t)
            if rec:
                for (l, h, s, v) in rec["w"]:
                    if l < hi and lo < h and v > waits.get(s, 0):
                        waits[s] = v
                for (l, h, s, v) in rec["r"]:
                    if l < hi and lo < h and v > waits.get(s, 0):
                        waits[s] = v
        for s, v in waits.items():
            self._wait(eng, s, v)
        sem = (dsem or DSEM[eng]) if dma else ESEM[eng]
        self.semnames.add(sem)
        k = 16 if dma else 1
        self.cnt[sem] += k
        val = self.cnt[sem]
        self.q[eng].append(("e", list(emitters), sem, k))
        for (t, lo, hi) in R:
            self.reg.setdefault(t, {"w": [], "r": []})["r"].append((lo, hi, sem, val))
        for (t, lo, hi) in W:
            self.reg.setdefault(t, {"w": [], "r": []})["w"].append((lo, hi, sem, val))
        return (sem, val)

    def barrier(self, skip=None):
        snap = [(s, v) for s, v in self.cnt.items() if v]
        for e in ENGS:
            sk = (skip or {}).get(e, ())
            for s, v in snap:
                if s in sk:
                    continue
                self._wait(e, s, v)
        self.reg = {}

    def finalize(self, block, sems):
        secmap = {"pe": block.tensor, "act": block.scalar, "dve": block.vector,
                  "pool": block.gpsimd, "sp": block.sync}
        for e in ENGS:
            ops = self.q[e]

            def body(E, ops=ops):
                for item in ops:
                    if item[0] == "w":
                        E.wait_ge(sems[item[1]], item[2])
                    else:
                        _, emitters, sem, k = item
                        n = len(emitters)
                        for i, f in enumerate(emitters):
                            inst = f(E)
                            if i == n - 1:
                                inst.then_inc(sems[sem], k)
            secmap[e](body)


class Stream:

    def __init__(self, P, name, tensor, width, nslots, srcs, queue="pool"):
        self.P, self.name, self.tensor = P, name, tensor
        self.width, self.nslots, self.srcs = width, nslots, srcs
        self.queue, self.next = queue, 0

    def issue(self, n=1):
        for _ in range(n):
            if self.next >= len(self.srcs):
                return
            i = self.next
            self.next += 1
            s = i % self.nslots
            lo, hi = s * self.width, (s + 1) * self.width
            src = self.srcs[i]
            if len(src.shape) == 3:
                dst = self.tensor[:, lo:lo + src.shape[1] * src.shape[2]]
                dst = dst.rearrange("p (s c) -> p s c", s=src.shape[1])
            else:
                dst = self.tensor[:, lo:lo + src.shape[-1]]
            self.P.grp(self.queue,
                       [lambda E, d=dst, s_=src: E.dma_start(d, s_)],
                       W=[(self.name, lo, hi)], dma=True,
                       dsem="d_%s%d" % (self.name, s))

    def slot(self, i):
        s = i % self.nslots
        return self.tensor[:, s * self.width:(s + 1) * self.width], \
            (self.name, s * self.width, (s + 1) * self.width)



def _colchunks(W, KC, MC):
    return np.ascontiguousarray(
        W.reshape(KC, 128, MC, 128).transpose(2, 1, 0, 3).reshape(MC, 128, KC * 128))


def prep_inputs(cfg, x, Wq, Wk, Wv, Wo, Wr, Wg, Wu, Wd):
    D, T, H, E, I = cfg["D"], cfg["T"], cfg["H"], cfg["E"], cfg["I"]
    KC, DC, IC, NG = D // 128, D // 128, I // 128, H // 4
    f32 = lambda a: np.asarray(a, np.float32)
    m = {}
    m["wq"] = _colchunks(f32(Wq), KC, H)
    m["wk"] = _colchunks(f32(Wk), KC, H)
    m["wv"] = _colchunks(f32(Wv), KC, H)
    wo = f32(Wo).reshape(NG, 4, 128, DC, 128)
    m["wo"] = np.ascontiguousarray(wo.transpose(0, 3, 2, 1, 4).reshape(NG, DC, 128, 512))
    m["wr"] = np.ascontiguousarray(
        f32(Wr).reshape(KC, 128, E).transpose(1, 0, 2).reshape(128, KC * E))
    m["wg"] = np.stack([_colchunks(f32(Wg[e]), KC, IC) for e in range(E)])
    m["wu"] = np.stack([_colchunks(f32(Wu[e]), KC, IC) for e in range(E)])
    m["wd"] = np.stack([_colchunks(f32(Wd[e]), IC, DC) for e in range(E)])
    m["cid32"] = np.eye(128, dtype=np.float32)
    m["ceps"] = np.full((128, 1), 1e-5, dtype=np.float32)
    m["cidbf"] = np.eye(128).astype(ml_dtypes.bfloat16)
    lower = (np.arange(128)[:, None] > np.arange(128)[None, :]).astype(np.float64)
    m["cmask"] = np.concatenate(
        [np.ones((128, 384)), lower, np.zeros((128, 384))], axis=1).astype(ml_dtypes.bfloat16)
    ccol = np.zeros((128, 512), dtype=ml_dtypes.bfloat16)
    ccol[:, 511] = 1.0
    m["ccol"] = ccol
    m["conesb"] = np.ones((128, 16), dtype=ml_dtypes.bfloat16)
    m["cones32"] = np.ones((1, 128), dtype=np.float32)
    m["cones1b"] = np.ones((1, 128), dtype=ml_dtypes.bfloat16)
    TC = T // 128
    m["ltci"] = (np.arange(128)[:, None] <= np.arange(128)[None, :]).astype(ml_dtypes.bfloat16)
    sel = np.zeros((128, 128), dtype=np.float32)
    for jp in range(TC):
        for ep in range(E):
            for jj in range(TC):
                if jp < jj:
                    sel[jp * 8 + ep, jj * 8 + ep] = 1.0
    m["sel"] = sel
    m["ctok"] = (np.arange(TC)[None, :] * 128 + np.arange(128)[:, None]).astype(np.int32)
    CP = max(256, ((5 * T // 16 + 127) // 128) * 128)
    ecap = np.tile((np.arange(E) * CP).astype(np.float32), T // 128)
    m["ecap"] = np.broadcast_to(ecap, (128, T // 128 * E)).copy()
    m["x"] = np.ascontiguousarray(f32(x))
    return m



def build_program(cfg, phases=("ln", "router", "att", "ffn", "final")):
    D, T, H, E, I = cfg["D"], cfg["T"], cfg["H"], cfg["E"], cfg["I"]
    KC, DC, IC = D // 128, D // 128, I // 128
    TC, QC, NG = T // 128, T // 512, H // 4
    assert H % 4 == 0 and T % 512 == 0 and D == I and E <= 16

    nc = bass.Bass("TRN2", target_bir_lowering=False)

    dx = nc.dram_tensor("x", [T, D], F32, kind="ExternalInput").ap()
    dwq = nc.dram_tensor("wq", [H, 128, D], F32, kind="ExternalInput").ap()
    dwk = nc.dram_tensor("wk", [H, 128, D], F32, kind="ExternalInput").ap()
    dwv = nc.dram_tensor("wv", [H, 128, D], F32, kind="ExternalInput").ap()
    dwo = nc.dram_tensor("wo", [NG, DC, 128, 512], F32, kind="ExternalInput").ap()
    dwr = nc.dram_tensor("wr", [128, KC * E], F32, kind="ExternalInput").ap()
    dwg = nc.dram_tensor("wg", [E, IC, 128, D], F32, kind="ExternalInput").ap()
    dwu = nc.dram_tensor("wu", [E, IC, 128, D], F32, kind="ExternalInput").ap()
    dwd = nc.dram_tensor("wd", [E, DC, 128, I], F32, kind="ExternalInput").ap()
    dcid32 = nc.dram_tensor("cid32", [128, 128], F32, kind="ExternalInput").ap()
    dceps = nc.dram_tensor("ceps", [128, 1], F32, kind="ExternalInput").ap()
    dcidbf = nc.dram_tensor("cidbf", [128, 128], BF16, kind="ExternalInput").ap()
    dcmask = nc.dram_tensor("cmask", [128, 896], BF16, kind="ExternalInput").ap()
    dccol = nc.dram_tensor("ccol", [128, 512], BF16, kind="ExternalInput").ap()
    dconesb = nc.dram_tensor("conesb", [128, 16], BF16, kind="ExternalInput").ap()
    dcones32 = nc.dram_tensor("cones32", [1, 128], F32, kind="ExternalInput").ap()
    dcones1b = nc.dram_tensor("cones1b", [1, 128], BF16, kind="ExternalInput").ap()
    dltci = nc.dram_tensor("ltci", [128, 128], BF16, kind="ExternalInput").ap()
    dsel = nc.dram_tensor("sel", [128, 128], F32, kind="ExternalInput").ap()
    dctok = nc.dram_tensor("ctok", [128, TC], I32, kind="ExternalInput").ap()
    decap = nc.dram_tensor("ecap", [128, TC * 8], F32, kind="ExternalInput").ap()
    dout = nc.dram_tensor("out", [T, D], F32, kind="ExternalOutput").ap()
    wdram = nc.dram_tensor("wdram", [E, T], F32).ap()
    CP = max(256, ((5 * T // 16 + 127) // 128) * 128)
    G = CP // 128
    xnd = nc.dram_tensor("xnd", [T, D], BF16).ap()
    idxdA = nc.dram_tensor("idxdA", [1, E * CP], I32).ap()
    idxdB = nc.dram_tensor("idxdB", [1, E * CP], I32).ap()

    P = Prog()
    es0 = ExitStack()
    sb = lambda n, shp, dt, st: st.enter_context(nc.sbuf_tensor(n, shp, dt))

    def ps(n, shp, dt, st):
        h = st.enter_context(nc.psum_tensor(n, shp, dt))
        P.psum_gran[n] = 2048 // (4 if dt == F32 else 2)
        return h

    es_xn = ExitStack()
    xn = es_xn.enter_context(nc.sbuf_tensor("xn", [128, KC * T], BF16, side="right"))
    acc = sb("acc", [128, DC * T], BF16, es0)
    cid32 = sb("cid32s", [128, 128], F32, es0)
    ceps = sb("cepss", [128, 1], F32, es0)
    nc.const_aps.register(F32, LN_EPS, ceps[:, 0:1])
    cidbf = sb("cidbfs", [128, 128], BF16, es0)
    cmask = sb("cmasks", [128, 896], BF16, es0)
    ccol = sb("ccols", [128, 512], BF16, es0)
    conesb = sb("conesbs", [128, 16], BF16, es0)
    cones32 = sb("cones32s", [1, 128], F32, es0)
    cones1b = sb("cones1bs", [1, 128], BF16, es0)
    ltci = sb("ltcis", [128, 128], BF16, es0)
    sel = sb("sels", [128, 128], F32, es0)
    ctok = sb("ctoks", [128, TC], I32, es0)
    ecapw = sb("ecaps", [128, TC * 8], F32, es0)
    posi = sb("posi", [128, 2 * TC], I32, es0)
    ws = sb("ws", [128, 2 * D], BF16, es0)
    wos = sb("wos", [128, 2 * 1024], BF16, es0)

    bregs = {}

    def _breg(E_, val):
        r = bregs.get(val)
        if r is None:
            r = E_.alloc_register()
            E_.reg_mov(r, val)
            bregs[val] = r
        return r

    for nm, dst, src in [("cid32", cid32, dcid32), ("ceps", ceps, dceps),
                         ("cidbf", cidbf, dcidbf),
                         ("cmask", cmask, dcmask), ("ccol", ccol, dccol),
                         ("conesb", conesb, dconesb), ("cones32", cones32, dcones32),
                         ("cones1b", cones1b, dcones1b), ("ltci", ltci, dltci),
                         ("sel", sel, dsel), ("ctok", ctok, dctok),
                         ("ecapw", ecapw, decap)]:
        P.grp("sp", [lambda E, d=dst, s=src: E.dma_start(d[:, :], s[:, :])],
              W=[(nm, 0, BIG)], dma=True, dsem="dCONST")
    P.barrier()

    es = ExitStack()
    ln_on = "ln" in phases
    xin = sb("xin", [128, 4 * D], F32, es)
    xtmp = sb("xtmp", [128, 2 * D], F32, es)
    stats = sb("stats", [128, TC * 12], F32, es)
    sqbuf = sb("sqbuf", [128, 512], F32, es)
    wrb = sb("wrb", [128, KC * E], BF16, es)
    wsb = sb("wsb", [8, T], BF16, es)
    Lrt = sb("Lrt", [128, TC * 8], F32, es)
    Lm = sb("Lm", [128, TC * 8], F32, es)
    eq1 = sb("eq1", [128, TC * 8], F32, es)
    eq2 = sb("eq2", [128, TC * 8], F32, es)
    tkp = sb("tkp", [128, TC * 8], F32, es)
    wtm = sb("wtm", [128, TC * 8], F32, es)
    flf = sb("flf", [128, TC * 8], F32, es)
    flg = sb("flg", [128, TC * 8], F32, es)
    flb = sb("flb", [128, TC * 8], BF16, es)
    incl = sb("incl", [128, TC * 8], F32, es)
    poss = sb("poss", [128, TC * 8], F32, es)
    ctrc = sb("ctrc", [128, 1], F32, es)
    crow = sb("crow", [1, 128], F32, es)
    orow = sb("orow", [1, 128], F32, es)
    idfill = sb("idfill", [128, E * CP // 128], I32, es)
    pos12 = sb("pos12", [128, TC * 2], F32, es)
    NJE = TC * 8
    pR = ps("pR", [128, 512], F32, es)
    pR2 = ps("pR2", [128, 512], F32, es)
    pRt = ps("pRt", [8, 128], F32, es)
    es_tr = ExitStack()
    ptrA = ps("ptrA", [128, 1024], F32, es_tr)
    ptrB = ps("ptrB", [128, 1024], F32, es_tr)
    if "router" in phases:
        P.grp("pool", [lambda E: E.dma_start(wrb[:, :], dwr[:, :])],
              W=[("wrb", 0, KC * E)], dma=True, dsem="dWRB")
        P.grp("pool", [lambda E_: E_.memset(idfill[:, :], 2 * T)],
              W=[("idfill", 0, BIG)])
        P.grp("pool", [lambda E_: E_.dma_start(idxdA[0:1, :], idfill[:, :])],
              R=[("idfill", 0, BIG)], W=[("idxdA", 0, BIG)], dma=True, dsem="dIDFA")
        P.grp("pool", [lambda E_: E_.dma_start(idxdB[0:1, :], idfill[:, :])],
              R=[("idfill", 0, BIG)], W=[("idxdB", 0, BIG)], dma=True, dsem="dIDFB")

    def router_j(j):
        pr, prn = (pR, "pR") if j % 2 == 0 else (pR2, "pR2")
        ems = [lambda E, o=pr[:, 0:E], l=xn[:, k * T + j * 128:k * T + (j + 1) * 128],
               r=wrb[:, k * E:(k + 1) * E], st=(k == 0), sp=(k == KC - 1):
               E.matmul(o, l, r, start=st, stop=sp) for k in range(KC)]
        P.grp("pe", ems, R=[("xn", 0, KC * T), ("wrb", 0, KC * E)], W=[(prn, 0, E)])
        RL = ("Lrt", j * 8, j * 8 + E)
        P.grp("dve", [lambda E, o=Lrt[:, j * 8:j * 8 + E], i=pr[:, 0:E]: E.tensor_copy(o, i)],
              R=[(prn, 0, E)], W=[RL])
        K_ = lambda t, i: (t, j * 8 + i, j * 8 + i + 1)
        Le = Lrt[:, j * 8:j * 8 + E]
        P.grp("dve", [lambda E, o=tkp[:, j * 8:j * 8 + 1], i=Le: E.reduce_max(o, i, axis=AX.X)],
              R=[RL], W=[K_("tkp", 0)])
        P.grp("dve", [lambda E, o=eq1[:, j * 8:j * 8 + E], i=Le, m=tkp[:, j * 8:j * 8 + 1]:
                      E.tensor_scalar(o, i, m, None, op0=ALU.is_equal)],
              R=[RL, K_("tkp", 0)], W=[("eq1", j * 8, j * 8 + E)])
        P.grp("dve", [lambda E, o=Lm[:, j * 8:j * 8 + E], i=eq1[:, j * 8:j * 8 + E], l=Le:
                      E.scalar_tensor_tensor(o, i, NEG_BIG, l, op0=ALU.mult, op1=ALU.add)],
              R=[("eq1", j * 8, j * 8 + E), RL], W=[("Lm", j * 8, j * 8 + E)])
        P.grp("dve", [lambda E, o=tkp[:, j * 8 + 1:j * 8 + 2], i=Lm[:, j * 8:j * 8 + E]:
                      E.reduce_max(o, i, axis=AX.X)],
              R=[("Lm", j * 8, j * 8 + E)], W=[K_("tkp", 1)])
        P.grp("dve", [lambda E, o=eq2[:, j * 8:j * 8 + E], i=Lm[:, j * 8:j * 8 + E],
                      m=tkp[:, j * 8 + 1:j * 8 + 2]:
                      E.tensor_scalar(o, i, m, None, op0=ALU.is_equal)],
              R=[("Lm", j * 8, j * 8 + E), K_("tkp", 1)], W=[("eq2", j * 8, j * 8 + E)])
        P.grp("dve", [lambda E, o=tkp[:, j * 8 + 2:j * 8 + 3], a=tkp[:, j * 8 + 1:j * 8 + 2],
                      b=tkp[:, j * 8:j * 8 + 1]: E.tensor_tensor(o, a, b, op=ALU.subtract)],
              R=[K_("tkp", 0), K_("tkp", 1)], W=[K_("tkp", 2)])
        P.grp("act", [lambda E, o=tkp[:, j * 8 + 3:j * 8 + 4], i=tkp[:, j * 8 + 2:j * 8 + 3]:
                      E.activation(o, i, AF.Sigmoid)], R=[K_("tkp", 2)], W=[K_("tkp", 3)])
        P.grp("act", [lambda E, o=tkp[:, j * 8 + 4:j * 8 + 5], i=tkp[:, j * 8 + 2:j * 8 + 3]:
                      E.activation(o, i, AF.Sigmoid, scale=-1.0)], R=[K_("tkp", 2)], W=[K_("tkp", 4)])
        P.grp("dve", [lambda E, o=wtm[:, j * 8:j * 8 + E], i=eq1[:, j * 8:j * 8 + E],
                      p=tkp[:, j * 8 + 4:j * 8 + 5]:
                      E.tensor_scalar(o, i, p, None, op0=ALU.mult)],
              R=[("eq1", j * 8, j * 8 + E), K_("tkp", 4)], W=[("wtm", j * 8, j * 8 + E)])
        P.grp("dve", [lambda E, o=Lm[:, j * 8:j * 8 + E], i=eq2[:, j * 8:j * 8 + E],
                      p=tkp[:, j * 8 + 3:j * 8 + 4]:
                      E.tensor_scalar(o, i, p, None, op0=ALU.mult)],
              R=[("eq2", j * 8, j * 8 + E), K_("tkp", 3)], W=[("Lm", j * 8, j * 8 + E)])
        P.grp("dve", [lambda E, o=wtm[:, j * 8:j * 8 + E], b=Lm[:, j * 8:j * 8 + E]:
                      E.tensor_tensor(o, o, b, op=ALU.add)],
              R=[("wtm", j * 8, j * 8 + E), ("Lm", j * 8, j * 8 + E)],
              W=[("wtm", j * 8, j * 8 + E)])
        P.grp("pe", [lambda E, o=pRt[0:E, 0:128], i=wtm[:, j * 8:j * 8 + E]:
                     E.transpose(o, i, cid32[:, :])],
              R=[("wtm", j * 8, j * 8 + E)], W=[("pRt", 0, 128)])
        P.grp("dve", [lambda E, o=wsb[0:E, j * 128:(j + 1) * 128], i=pRt[0:E, 0:128]:
                      E.tensor_copy(o, i)],
              R=[("pRt", 0, 128)], W=[("wsb", j * 128, (j + 1) * 128)])
        P.grp("dve", [lambda E_, o=flf[:, j * 8:j * 8 + E], a=eq1[:, j * 8:j * 8 + E],
                      b=eq2[:, j * 8:j * 8 + E]: E_.tensor_tensor(o, a, b, op=ALU.add)],
              R=[("eq1", j * 8, j * 8 + E), ("eq2", j * 8, j * 8 + E)],
              W=[("flf", j * 8, j * 8 + E)])

    def x_load(j):
        slot = j % 4
        P.grp("sp", [lambda E, o=xin[:, slot * D:(slot + 1) * D],
                     i=dx[j * 128:(j + 1) * 128, :]: E.dma_start(o, i)],
              W=[("xin", slot * D, (slot + 1) * D)], dma=True, dsem="dXIN%d" % slot)

    def ln_a(j):
        slot = j % 4
        xi = xin[:, slot * D:(slot + 1) * D]
        Rxi = ("xin", slot * D, (slot + 1) * D)
        S = lambda i: stats[:, j * 12 + i:j * 12 + i + 1]
        Sr = lambda i: ("stats", j * 12 + i, j * 12 + i + 1)
        for q4 in range(D // 512):
            c_ = 0 if q4 == 0 else 7 + q4
            P.grp("act", [lambda E, i=xi[:, q4 * 512:(q4 + 1) * 512], a=S(c_):
                          E.activation(sqbuf[:, 0:512], i, AF.Square, accum_out=a)],
                  R=[Rxi], W=[("sqbuf", 0, 512), Sr(c_)])
        P.grp("dve", [lambda E, i=xi, o=S(1): E.reduce_sum(o, i, axis=AX.X)],
              R=[Rxi], W=[Sr(1)])

    def ln_b(j):
        S = lambda i: stats[:, j * 12 + i:j * 12 + i + 1]
        Sr = lambda i: ("stats", j * 12 + i, j * 12 + i + 1)
        if D > 512:
            P.grp("dve", [lambda E, o=S(0), a=S(8): E.tensor_tensor(o, o, a, op=ALU.add)],
                  R=[Sr(0), Sr(8)], W=[Sr(0)])
            P.grp("dve", [lambda E, o=S(9), a=S(10): E.tensor_tensor(o, o, a, op=ALU.add)],
                  R=[Sr(9), Sr(10)], W=[Sr(9)])
            P.grp("dve", [lambda E, o=S(0), a=S(9): E.tensor_tensor(o, o, a, op=ALU.add)],
                  R=[Sr(0), Sr(9)], W=[Sr(0)])
        P.grp("dve", [lambda E, o=S(2), i=S(1): E.tensor_scalar(o, i, 1.0 / D, None, op0=ALU.mult)],
              R=[Sr(1)], W=[Sr(2)])
        P.grp("dve", [lambda E, o=S(6), i=S(2): E.tensor_tensor(o, i, i, op=ALU.mult)],
              R=[Sr(2)], W=[Sr(6)])
        P.grp("dve", [lambda E, o=S(3), i=S(0), m2=S(6):
                      E.scalar_tensor_tensor(o, i, 1.0 / D, m2, op0=ALU.mult, op1=ALU.subtract)],
              R=[Sr(0), Sr(6)], W=[Sr(3)])
        P.grp("act", [lambda E, o=S(4), i=S(3): E.activation(o, i, AF.Ln, bias=LN_EPS)],
              R=[Sr(3)], W=[Sr(4)])
        P.grp("act", [lambda E, o=S(5), i=S(4): E.activation(o, i, AF.Exp, scale=-0.5)],
              R=[Sr(4)], W=[Sr(5)])

    def ln_c(j):
        slot = j % 4
        xi = xin[:, slot * D:(slot + 1) * D]
        Rxi = ("xin", slot * D, (slot + 1) * D)
        S = lambda i: stats[:, j * 12 + i:j * 12 + i + 1]
        Sr = lambda i: ("stats", j * 12 + i, j * 12 + i + 1)
        ts_ = (j % 2)
        xt = xtmp[:, ts_ * D:(ts_ + 1) * D]
        P.grp("dve", [lambda E, o=xt, i=xi, m=S(2), r=S(5):
                      E.tensor_scalar(o, i, m, r, op0=ALU.subtract, op1=ALU.mult)],
              R=[Rxi, Sr(2), Sr(5)], W=[("xtmp", ts_ * D, (ts_ + 1) * D)])
        P.grp("pool", [lambda E, o=xnd[j * 128:(j + 1) * 128, :], i=xt:
                       E.dma_start(o, i)],
              R=[("xtmp", ts_ * D, (ts_ + 1) * D)], W=[("xnd", j * D, (j + 1) * D)],
              dma=True, dsem="dXND%d" % ts_)
        for k in range(KC):
            pt, pn = (ptrA, "ptrA") if (k % 4) < 2 else (ptrB, "ptrB")
            off = (k % 2) * 512 + ((k // 4) % 4) * 128
            P.grp("pe", [lambda E, o=pt[:, off:off + 128],
                         i=xtmp[:, ts_ * D + k * 128:ts_ * D + (k + 1) * 128]:
                         E.transpose(o, i, cid32[:, :])],
                  R=[("xtmp", ts_ * D + k * 128, ts_ * D + (k + 1) * 128)],
                  W=[(pn, off, off + 128)])
            P.grp("dve", [lambda E, o=xn[:, k * T + j * 128:k * T + (j + 1) * 128],
                          i=pt[:, off:off + 128]: E.tensor_copy(o, i)],
                  R=[(pn, off, off + 128)],
                  W=[("xn", k * T + j * 128, k * T + (j + 1) * 128)])

    if ln_on:
        x_load(0)
    for jj in range(TC + 3 if ln_on else 0):
        if jj < TC:
            if jj + 1 < TC:
                x_load(jj + 1)
            ln_a(jj)
        if 1 <= jj and jj - 1 < TC:
            ln_b(jj - 1)
        if 2 <= jj and jj - 2 < TC:
            ln_c(jj - 2)
        if 3 <= jj and jj - 3 < TC and "router" in phases:
            router_j(jj - 3)
    es_tr.close()

    qkv_srcs = []
    for h in range(H):
        qkv_srcs += [dwq[h], dwk[h], dwv[h]]
    st_qkv = Stream(P, "ws", ws, D, 2, qkv_srcs)
    wo_srcs = []
    for hgi in range(NG):
        for m in range(0, DC, 2):
            wo_srcs.append(dwo[hgi, m:m + 2].rearrange("s p c -> p s c"))
    st_wo = Stream(P, "wos", wos, 1024, 2, wo_srcs)
    if "att" in phases:
        st_qkv.issue(1)
        st_wo.issue(1)

    if "router" in phases:
        P.grp("pool", [lambda E_, w=wsb[0:E, :]: E_.dma_start(wdram[:, :], w)],
              R=[("wsb", 0, T)], W=[("wdram", 0, E * T)], dma=True, dsem="dWDR")
    if "router" in phases:
        pX = ps("pX", [128, 128], F32, es)
        pY = ps("pY", [1, 128], F32, es)
        pZ = ps("pZ", [128, 128], F32, es)
        idxfA = idxdA[0:1, :].unsqueeze(2)
        idxfB = idxdB[0:1, :].unsqueeze(2)
        P.grp("dve", [lambda E_: E_.tensor_scalar(flg[:, 0:NJE], flf[:, 0:NJE], 0.0, None,
                                                  op0=ALU.is_gt)],
              R=[("flf", 0, NJE)], W=[("flg", 0, NJE)])
        P.grp("dve", [lambda E_: E_.tensor_copy(flb[:, 0:NJE], flg[:, 0:NJE])],
              R=[("flg", 0, NJE)], W=[("flb", 0, NJE)])
        P.grp("pe", [lambda E_, c=c: E_.matmul(pX[:, c:c + 1], ltci[:, :], flb[:, c:c + 1],
                                               start=True, stop=True) for c in range(NJE)],
              R=[("flb", 0, NJE), ("ltci", 0, 128)], W=[("pX", 0, 128)])
        P.grp("dve", [lambda E_: E_.tensor_copy(incl[:, 0:NJE], pX[:, 0:NJE])],
              R=[("pX", 0, 128)], W=[("incl", 0, NJE)])
        P.grp("dve", [lambda E_: E_.tensor_tensor(incl[:, 0:NJE], incl[:, 0:NJE],
                                                  flg[:, 0:NJE], op=ALU.subtract)],
              R=[("incl", 0, NJE), ("flg", 0, NJE)], W=[("incl", 0, NJE)])
        P.grp("dve", [lambda E_: E_.memset(crow[0:1, :], 0.0)], W=[("crow", 0, 128)])
        P.grp("pe", [lambda E_, c=c: E_.matmul(pY[0:1, c:c + 1], conesb[:, 0:1],
                                               flb[:, c:c + 1], start=True, stop=True)
                     for c in range(NJE)],
              R=[("flb", 0, NJE), ("conesb", 0, 1)], W=[("pY", 0, 128)])
        P.grp("act", [lambda E_: E_.copy(crow[0:1, 0:NJE], pY[0:1, 0:NJE])],
              R=[("pY", 0, 128)], W=[("crow", 0, NJE)])
        P.grp("pe", [lambda E_: E_.transpose(pZ[:, 0:1], crow[0:1, 0:128], cid32[0:1, 0:1])],
              R=[("crow", 0, 128)], W=[("pZ", 0, 128)])
        P.grp("dve", [lambda E_: E_.tensor_copy(ctrc[:, 0:1], pZ[:, 0:1])],
              R=[("pZ", 0, 128)], W=[("ctrc", 0, 1)])
        P.grp("pe", [lambda E_: E_.matmul(pY[0:1, 0:128], ctrc[:, 0:1], sel[:, :],
                                          start=True, stop=True)],
              R=[("ctrc", 0, 1), ("sel", 0, 128)], W=[("pY", 0, 128)])
        P.grp("act", [lambda E_: E_.copy(orow[0:1, :], pY[0:1, 0:128])],
              R=[("pY", 0, 128)], W=[("orow", 0, 128)])
        P.grp("pe", [lambda E_: E_.matmul(pZ[:, 0:128], cones32[0:1, 0:128],
                                          orow[0:1, 0:128], start=True, stop=True)],
              R=[("orow", 0, 128), ("cones32", 0, 128)], W=[("pZ", 0, 128)])
        P.grp("dve", [lambda E_: E_.tensor_tensor(poss[:, 0:NJE], incl[:, 0:NJE],
                                                  pZ[:, 0:NJE], op=ALU.add)],
              R=[("incl", 0, NJE), ("pZ", 0, 128)], W=[("poss", 0, NJE)])
        P.grp("dve", [lambda E_: E_.tensor_scalar(incl[:, 0:NJE], poss[:, 0:NJE],
                                                  float(CP - 1), 2.0e6, op0=ALU.is_gt,
                                                  op1=ALU.mult)],
              R=[("poss", 0, NJE)], W=[("incl", 0, NJE)])
        P.grp("dve", [lambda E_: E_.tensor_tensor(poss[:, 0:NJE], poss[:, 0:NJE],
                                                  incl[:, 0:NJE], op=ALU.add)],
              R=[("poss", 0, NJE), ("incl", 0, NJE)], W=[("poss", 0, NJE)])
        P.grp("dve", [lambda E_: E_.tensor_tensor(poss[:, 0:NJE], poss[:, 0:NJE],
                                                  ecapw[:, 0:NJE], op=ALU.add)],
              R=[("poss", 0, NJE), ("ecapw", 0, NJE)], W=[("poss", 0, NJE)])
        for j in range(TC):
            P.grp("dve", [lambda E_, o=incl[:, j * 8:j * 8 + E], a=poss[:, j * 8:j * 8 + E],
                          b=eq1[:, j * 8:j * 8 + E]: E_.tensor_tensor(o, a, b, op=ALU.mult)],
                  R=[("poss", j * 8, j * 8 + E), ("eq1", j * 8, j * 8 + E)],
                  W=[("incl", j * 8, j * 8 + E)])
            P.grp("dve", [lambda E_, o=pos12[:, 2 * j:2 * j + 1], i=incl[:, j * 8:j * 8 + E]:
                          E_.reduce_sum(o, i, axis=AX.X)],
                  R=[("incl", j * 8, j * 8 + E)], W=[("pos12", 2 * j, 2 * j + 1)])
            P.grp("dve", [lambda E_, o=incl[:, j * 8:j * 8 + E], a=poss[:, j * 8:j * 8 + E],
                          b=eq2[:, j * 8:j * 8 + E]: E_.tensor_tensor(o, a, b, op=ALU.mult)],
                  R=[("poss", j * 8, j * 8 + E), ("eq2", j * 8, j * 8 + E)],
                  W=[("incl", j * 8, j * 8 + E)])
            P.grp("dve", [lambda E_, o=pos12[:, 2 * j + 1:2 * j + 2], i=incl[:, j * 8:j * 8 + E]:
                          E_.reduce_sum(o, i, axis=AX.X)],
                  R=[("incl", j * 8, j * 8 + E)], W=[("pos12", 2 * j + 1, 2 * j + 2)])
        P.grp("dve", [lambda E_: E_.tensor_copy(posi[:, 0:2 * TC], pos12[:, 0:2 * TC])],
              R=[("pos12", 0, 2 * TC)], W=[("posi", 0, 2 * TC)])
        for j in range(TC):
            for s2, tgt, tn in ((0, idxfA, "idxdA"), (1, idxfB, "idxdB")):
                P.grp("pool",
                      [lambda E_, po=posi[:, 2 * j + s2:2 * j + s2 + 1], ti=ctok[:, j:j + 1],
                       t_=tgt: E_.indirect_dma_start(t_, bass.IndirectOffsetOnAxis(ap=po, axis=1),
                                                     ti, None,
                                                     bounds_check=_breg(E_, E * CP - 1),
                                                     oob_is_err=False)],
                      R=[("posi", 2 * j + s2, 2 * j + s2 + 1), ("ctok", j, j + 1)],
                      W=[(tn, 0, BIG)], dma=True, dsem="dIDX%d" % s2)
    es.close()
    _sk = {"dIDX0", "dIDX1"}
    P.barrier(skip={"pe": _sk, "act": _sk, "dve": _sk})

    es = ExitStack()
    qbf = sb("qbf", [128, T], BF16, es)
    kbf = sb("kbf", [128, T], BF16, es)
    v32 = sb("v32", [128, T], F32, es)
    vtm = sb("vtm", [128, T], BF16, es)
    expT = sb("expT", [128, TC * 512], BF16, es)
    attg = sb("attg", [128, 4 * T], BF16, es)
    srow = sb("srow", [1, 512], F32, es)
    srow2 = sb("srow2", [1, 512], F32, es)
    recrow = sb("recrow", [1, 1024], F32, es)
    recip = sb("recip", [128, 512], F32, es)
    pLa = ps("pLa", [128, 1024], F32, es)
    pLb = ps("pLb", [128, 1024], F32, es)
    pAV = ps("pAV", [128, 512], F32, es)
    pAV2 = ps("pAV2", [128, 512], F32, es)
    pS = ps("pS", [128, 512], F32, es)
    pS2 = ps("pS2", [128, 512], F32, es)

    QKV_PS = [(pLa, "pLa", 0), (pLa, "pLa", 512), (pLb, "pLb", 0), (pLb, "pLb", 512)]
    VT_PS = [(pAV, "pAV"), (pAV2, "pAV2"), (pS, "pS"), (pS2, "pS2")]
    att_pending = [None]

    for h in range(H if "att" in phases else 0):
        if att_pending[0] is not None:
            att_pending[0]()
            att_pending[0] = None
        for pi, (dst, dn, deng) in enumerate([(qbf, "qbf", "act"), (kbf, "kbf", "act"),
                                              (v32, "v32", "dve")]):
            st_qkv.issue(1)
            wsl, wreg = st_qkv.slot(3 * h + pi)
            for n in range(QC):
                pt, pn, po = QKV_PS[n]
                ems = [lambda E, o=pt[:, po:po + 512], l=wsl[:, k * 128:(k + 1) * 128],
                       r=xn[:, k * T + n * 512:k * T + n * 512 + 512],
                       st=(k == 0), sp=(k == KC - 1):
                       E.matmul(o, l, r, start=st, stop=sp) for k in range(KC)]
                P.grp("pe", ems, R=[wreg, ("xn", 0, KC * T)], W=[(pn, po, po + 512)])
                if deng == "act":
                    P.grp("act", [lambda E, o=dst[:, n * 512:(n + 1) * 512],
                                  i=pt[:, po:po + 512]: E.activation(o, i, AF.Copy)],
                          R=[(pn, po, po + 512)], W=[(dn, n * 512, (n + 1) * 512)])
                else:
                    P.grp("dve", [lambda E, o=dst[:, n * 512:(n + 1) * 512],
                                  i=pt[:, po:po + 512]: E.tensor_copy(o, i)],
                          R=[(pn, po, po + 512)], W=[(dn, n * 512, (n + 1) * 512)])
        for r in range(TC):
            pt, pn = VT_PS[r % 4]
            off = 128 * ((r // 4) % 4)
            P.grp("pe", [lambda E, o=pt[:, off:off + 128].bitcast(F32) if False else pt[:, off:off + 128],
                         i=v32[:, r * 128:(r + 1) * 128]: E.transpose(o, i, cid32[:, :])],
                  R=[("v32", r * 128, (r + 1) * 128)], W=[(pn, off, off + 128)])
            P.grp("dve", [lambda E, o=vtm[:, r * 128:(r + 1) * 128], i=pt[:, off:off + 128]:
                          E.tensor_copy(o, i)],
                  R=[(pn, off, off + 128)], W=[("vtm", r * 128, (r + 1) * 128)])
        for c in range(QC):
            real = list(range(4 * c, TC))
            cmcol = list(range(0, 4 * c)) if c == QC - 1 else []
            pi = 0
            idx = 0
            while idx < len(real):
                take = 2 if idx + 1 < len(real) else 1
                rs = real[idx:idx + take]
                pt, pn = (pLa, "pLa") if pi % 2 == 0 else (pLb, "pLb")
                ems = [lambda E, o=pt[:, u * 512:(u + 1) * 512],
                       l=kbf[:, r * 128:(r + 1) * 128],
                       r_=qbf[:, c * 512:(c + 1) * 512]:
                       E.matmul(o, l, r_, start=True, stop=True)
                       for u, r in enumerate(rs)]
                P.grp("pe", ems,
                      R=[("kbf", 0, T), ("qbf", c * 512, (c + 1) * 512)],
                      W=[(pn, 0, take * 512)])
                P.grp("act", [lambda E, o=expT[:, rs[0] * 512:(rs[0] + take) * 512],
                              i=pt[:, 0:take * 512]:
                              E.activation(o, i, AF.Exp, scale=EXP_SCALE)],
                      R=[(pn, 0, take * 512)],
                      W=[("expT", rs[0] * 512, (rs[0] + take) * 512)])
                pi += 1
                idx += take
            for r in range(4 * c, min(4 * c + 4, TC)):
                dlt = r - 4 * c
                P.grp("dve", [lambda E, o=expT[:, r * 512:(r + 1) * 512],
                              m=cmask[:, 384 - 128 * dlt:384 - 128 * dlt + 512]:
                              E.tensor_tensor(o, o, m, op=ALU.mult)],
                      R=[("expT", r * 512, (r + 1) * 512), ("cmask", 0, 896)],
                      W=[("expT", r * 512, (r + 1) * 512)])
            if c == QC - 1:
                for r in real:
                    P.grp("pool", [lambda E, o=expT[:, r * 512 + 511:r * 512 + 512]:
                                   E.memset(o, 1.0)],
                          W=[("expT", r * 512 + 511, r * 512 + 512)])
            pSx, pSn = (pS, "pS") if c % 2 == 0 else (pS2, "pS2")
            pAx, pAn = (pAV, "pAV") if c % 2 == 0 else (pAV2, "pAV2")
            cp_ = (c % 2) * 512
            rhs_list = [(expT[:, r * 512:(r + 1) * 512], ("expT", r * 512, (r + 1) * 512))
                        for r in real]
            rhs_list += [(ccol[:, 0:512], ("ccol", 0, 512)) for _ in cmcol]
            ems = [lambda E, o=pSx[0:1, 0:512], r_=rhs[0], st=(i == 0),
                   sp=(i == len(rhs_list) - 1):
                   E.matmul(o, conesb[:, 0:1], r_, start=st, stop=sp)
                   for i, rhs in enumerate(rhs_list)]
            P.grp("pe", ems, R=[rr[1] for rr in rhs_list] + [("conesb", 0, 1)],
                  W=[(pSn, 0, 512)])
            P.grp("act", [lambda E, o=srow[0:1, 0:512], i=pSx[0:1, 0:512]:
                          E.copy(o, i)],
                  R=[(pSn, 0, 512)], W=[("srow", 0, 512)])
            P.grp("act", [lambda E, o=srow2[0:1, 0:512], i=srow[0:1, 0:512]:
                          E.activation(o, i, AF.Ln)],
                  R=[("srow", 0, 512)], W=[("srow2", 0, 512)])
            P.grp("act", [lambda E, o=recrow[0:1, cp_:cp_ + 512], i=srow2[0:1, 0:512]:
                          E.activation(o, i, AF.Exp, scale=-1.0)],
                  R=[("srow2", 0, 512)], W=[("recrow", cp_, cp_ + 512)])
            vt_list = [vtm[:, r * 128:(r + 1) * 128] for r in real] + \
                      [vtm[:, r * 128:(r + 1) * 128] for r in cmcol]
            ems = [lambda E, o=pAx[:, 0:512], l=vt_list[i], r_=rhs_list[i][0],
                   st=(i == 0), sp=(i == len(rhs_list) - 1):
                   E.matmul(o, l, r_, start=st, stop=sp)
                   for i in range(len(rhs_list))]
            P.grp("pe", ems, R=[rr[1] for rr in rhs_list] + [("vtm", 0, T)],
                  W=[(pAn, 0, 512)])
            if att_pending[0] is not None:
                att_pending[0]()
            g_in = h % 4

            def epilogue(pSx=pSx, pSn=pSn, pAx=pAx, pAn=pAn, cp_=cp_, g_in=g_in, c=c):
                P.grp("pe", [lambda E, o=pSx[:, 0:512], r_=recrow[0:1, cp_:cp_ + 512]:
                             E.matmul(o, cones32[0:1, 0:128], r_, start=True, stop=True)],
                      R=[("recrow", cp_, cp_ + 512), ("cones32", 0, 128)], W=[(pSn, 0, 512)])
                P.grp("dve", [lambda E, o=recip[:, 0:512], i=pSx[:, 0:512]:
                              E.tensor_copy(o, i)],
                      R=[(pSn, 0, 512)], W=[("recip", 0, 512)])
                P.grp("dve", [lambda E, o=attg[:, g_in * T + c * 512:g_in * T + (c + 1) * 512],
                              a=pAx[:, 0:512], b=recip[:, 0:512]:
                              E.tensor_tensor(o, a, b, op=ALU.mult)],
                      R=[(pAn, 0, 512), ("recip", 0, 512)],
                      W=[("attg", g_in * T + c * 512, g_in * T + (c + 1) * 512)])
            att_pending[0] = epilogue
        if h % 4 == 3:
            if att_pending[0] is not None:
                att_pending[0]()
                att_pending[0] = None
            HGI = h // 4
            for m in range(DC):
                if m % 2 == 0:
                    st_wo.issue(1)
                wsl, wreg = st_wo.slot(HGI * (DC // 2) + m // 2)
                mo = (m % 2) * 512
                for n in range(QC):
                    pt, pn, po = QKV_PS[n] if m % 2 == 0 else \
                        [(pAV, "pAV", 0), (pAV2, "pAV2", 0), (pS, "pS", 0), (pS2, "pS2", 0)][n]
                    ems = [lambda E, o=pt[:, po:po + 512],
                           l=wsl[:, mo + ks * 128:mo + (ks + 1) * 128],
                           r=attg[:, ks * T + n * 512:ks * T + n * 512 + 512],
                           st=(ks == 0), sp=(ks == 3):
                           E.matmul(o, l, r, start=st, stop=sp) for ks in range(4)]
                    P.grp("pe", ems, R=[wreg, ("attg", 0, 4 * T)], W=[(pn, po, po + 512)])
                    arange = ("acc", m * T + n * 512, m * T + (n + 1) * 512)
                    if HGI == 0:
                        P.grp("dve", [lambda E, o=acc[:, m * T + n * 512:m * T + (n + 1) * 512],
                                      i=pt[:, po:po + 512]: E.tensor_copy(o, i)],
                              R=[(pn, po, po + 512)], W=[arange])
                    else:
                        P.grp("dve", [lambda E, o=acc[:, m * T + n * 512:m * T + (n + 1) * 512],
                                      i=pt[:, po:po + 512]:
                                      E.tensor_tensor(o, i, o, op=ALU.add)],
                              R=[(pn, po, po + 512), arange], W=[arange])
    es.close()
    P.barrier()

    es_xn.close()

    es = ExitStack()
    xin2 = sb("xin2", [128, D], F32, es)
    osb = sb("osb", [128, D], F32, es)
    fs = sb("fs", [128, 4 * D], BF16, es)
    pA = ps("pA", [128, 1024], F32, es)
    pB = ps("pB", [128, 1024], F32, es)
    pTa2 = ps("pTa2", [128, 1024], BF16, es)
    pTb2 = ps("pTb2", [128, 1024], BF16, es)
    pTc2 = ps("pTc2", [128, 1024], BF16, es)
    pTd2 = ps("pTd2", [128, 1024], BF16, es)
    PT4 = [(pTa2, "pTa2"), (pTb2, "pTb2"), (pTc2, "pTc2"), (pTd2, "pTd2")]
    xnef = sb("xnef", [128, KC * CP], BF16, es)
    gbuf = sb("gbuf", [128, IC * CP], BF16, es)
    hfm = sb("hfm", [128, DC * CP], BF16, es)
    XGS = min(G, 4)
    xgat = sb("xgat", [128, XGS * D], BF16, es)
    htm = sb("htm", [128, 2 * D], F32, es)
    idxs = sb("idxs", [128, 2 * G], I32, es)
    idxab = sb("idxab", [128, 4 * G], I32, es)
    wg_s = sb("wg_s", [128, 2 * G], F32, es)
    xndf = xnd.unsqueeze(0)
    doutf = dout.unsqueeze(0)
    wdf = wdram.rearrange("e t -> (e t)").unsqueeze(0).unsqueeze(2)

    def residual_load(j):
        P.grp("sp", [lambda E_, o=xin2[:, :],
                     i=dx[j * 128:(j + 1) * 128, :]: E_.dma_start(o, i)],
              W=[("xin2", 0, D)], dma=True, dsem="dXIN0")

    def residual_comp(j):
        for m in range(DC):
            pt, pn = PT4[m % 4]
            off = ((m // 4) % 8) * 128
            P.grp("pe", [lambda E_, o=pt[:, off:off + 128],
                         i=acc[:, m * T + j * 128:m * T + (j + 1) * 128]:
                         E_.transpose(o, i, cidbf[:, :])],
                  R=[("acc", m * T + j * 128, m * T + (j + 1) * 128)],
                  W=[(pn, off, off + 128)])
            P.grp("dve", [lambda E_, o=osb[:, m * 128:(m + 1) * 128],
                          a=pt[:, off:off + 128],
                          b=xin2[:, m * 128:(m + 1) * 128]:
                          E_.tensor_tensor(o, a, b, op=ALU.add)],
                  R=[(pn, off, off + 128), ("xin2", m * 128, (m + 1) * 128)],
                  W=[("osb", m * 128, (m + 1) * 128)])
        P.grp("sp", [lambda E_, o=dout[j * 128:(j + 1) * 128, :], i=osb[:, :]:
                     E_.dma_start(o, i)],
              R=[("osb", 0, D)], W=[("out", j * D, (j + 1) * D)],
              dma=True, dsem="dOUT0")

    if "final" in phases and "ffn" not in phases:
        for j in range(TC):
            residual_load(j)
            residual_comp(j)

    ffn_srcs = []
    for e in range(E):
        for m in range(IC):
            ffn_srcs.append(dwg[e, m])
        for m in range(IC):
            ffn_srcs.append(dwu[e, m])
        for m in range(DC):
            ffn_srcs.append(dwd[e, m])
    st_f = Stream(P, "fs", fs, D, 4, ffn_srcs)
    item = 0

    NT = [(i * 512, min(512, CP - i * 512)) for i in range((CP + 511) // 512)]
    EV_PS = [(pA, "pA", 0), (pA, "pA", 512)]
    OD_PS = [(pB, "pB", 0), (pB, "pB", 512)]

    def fetch_expert_dmas(e):
        sl = e % 2
        P.grp("pool", [lambda E_, o=idxab[:, sl * 2 * G:sl * 2 * G + G],
                       i=idxdA[0:1, e * CP:(e + 1) * CP].rearrange("a (g p) -> (a p) g", p=128):
                       E_.dma_start(o, i, allow_slow_non_contiguous=True)],
              R=[("idxdA", 0, BIG)], W=[("idxab", sl * 2 * G, sl * 2 * G + G)],
              dma=True, dsem="dIXL%d" % sl)
        P.grp("pool", [lambda E_, o=idxab[:, sl * 2 * G + G:sl * 2 * G + 2 * G],
                       i=idxdB[0:1, e * CP:(e + 1) * CP].rearrange("a (g p) -> (a p) g", p=128):
                       E_.dma_start(o, i, allow_slow_non_contiguous=True)],
              R=[("idxdB", 0, BIG)], W=[("idxab", sl * 2 * G + G, sl * 2 * G + 2 * G)],
              dma=True, dsem="dIXM%d" % sl)
        P.grp("dve", [lambda E_, o=idxs[:, sl * G:(sl + 1) * G],
                      a=idxab[:, sl * 2 * G:sl * 2 * G + G],
                      b=idxab[:, sl * 2 * G + G:sl * 2 * G + 2 * G]:
                      E_.tensor_tensor(o, a, b, op=ALU.min)],
              R=[("idxab", sl * 2 * G, sl * 2 * G + 2 * G)],
              W=[("idxs", sl * G, (sl + 1) * G)])
        for g in range(G):
            P.grp("pool", [lambda E_, o=wg_s[:, sl * G + g:sl * G + g + 1],
                           po=idxs[:, sl * G + g:sl * G + g + 1], eo=e * T:
                           E_.indirect_dma_start(o, None, wdf,
                                                 bass.IndirectOffsetOnAxis(ap=po, axis=1),
                                                 element_offset=eo,
                                                 bounds_check=_breg(E_, T - 1), oob_is_err=False)],
                  R=[("idxs", sl * G + g, sl * G + g + 1), ("wdram", e * T, (e + 1) * T)],
                  W=[("wg_s", sl * G + g, sl * G + g + 1)], dma=True,
                  dsem="dWG%d_%d" % (sl, g))
            if g < XGS:
                xn_gather(e, g)

    def xn_gather(e, g):
        sl = e % 2
        P.grp("pool", [lambda E_, o=xgat[:, (g % XGS) * D:(g % XGS + 1) * D],
                       po=idxs[:, sl * G + g:sl * G + g + 1]:
                       E_.indirect_dma_start(o, None, xndf,
                                             bass.IndirectOffsetOnAxis(ap=po, axis=1),
                                             bounds_check=_breg(E_, T - 1), oob_is_err=False)],
              R=[("idxs", sl * G + g, sl * G + g + 1), ("xnd", 0, BIG)],
              W=[("xgat", (g % XGS) * D, (g % XGS + 1) * D)], dma=True,
              dsem="dXG%d" % (g % XGS))

    def fetch_expert_transposes(e):
        sl = e % 2
        for g in range(G):
            for k in range(KC):
                pt, pn = PT4[k % 4]
                off = ((k // 4) % 8) * 128
                P.grp("pe", [lambda E_, o=pt[:, off:off + 128],
                             i=xgat[:, (g % XGS) * D + k * 128:(g % XGS) * D + (k + 1) * 128]:
                             E_.transpose(o, i, cidbf[:, :])],
                      R=[("xgat", (g % XGS) * D + k * 128, (g % XGS) * D + (k + 1) * 128)],
                      W=[(pn, off, off + 128)])
                P.grp("dve", [lambda E_, o=xnef[:, k * CP + g * 128:k * CP + (g + 1) * 128],
                              i=pt[:, off:off + 128]: E_.tensor_copy(o, i)],
                      R=[(pn, off, off + 128)],
                      W=[("xnef", k * CP + g * 128, k * CP + (g + 1) * 128)])
            if g + XGS < G:
                xn_gather(e, g + XGS)

    for e in range(E if "ffn" in phases else 0):
        if e == 0:
            fetch_expert_dmas(0)
            st_f.issue(3)
            if "final" in phases:
                residual_load(0)
            fetch_expert_transposes(0)
        for m in range(IC):
            st_f.issue(1)
            wsl, wreg = st_f.slot(item); item += 1
            PSU = EV_PS if m % 2 == 0 else OD_PS
            for ni, (no, nw) in enumerate(NT):
                pt, pn, po = PSU[ni]
                ems = [lambda E_, o=pt[:, po:po + nw], l=wsl[:, k * 128:(k + 1) * 128],
                       r=xnef[:, k * CP + no:k * CP + no + nw],
                       st=(k == 0), sp=(k == KC - 1):
                       E_.matmul(o, l, r, start=st, stop=sp) for k in range(KC)]
                P.grp("pe", ems, R=[wreg, ("xnef", 0, BIG)], W=[(pn, po, po + nw)])
                gr = ("gbuf", m * CP + no, m * CP + no + nw)
                P.grp("act", [lambda E_, o=gbuf[:, m * CP + no:m * CP + no + nw],
                              i=pt[:, po:po + nw]: E_.activation(o, i, AF.Silu)],
                      R=[(pn, po, po + nw)], W=[gr])
            if e == 0 and "final" in phases and m < TC:
                residual_comp(m)
                if m + 1 < TC:
                    residual_load(m + 1)
        if e + 1 < E:
            fetch_expert_dmas(e + 1)
        for m in range(IC):
            st_f.issue(1)
            wsl, wreg = st_f.slot(item); item += 1
            PSU = EV_PS if m % 2 == 0 else OD_PS
            for ni, (no, nw) in enumerate(NT):
                pt, pn, po = PSU[ni]
                ems = [lambda E_, o=pt[:, po:po + nw], l=wsl[:, k * 128:(k + 1) * 128],
                       r=xnef[:, k * CP + no:k * CP + no + nw],
                       st=(k == 0), sp=(k == KC - 1):
                       E_.matmul(o, l, r, start=st, stop=sp) for k in range(KC)]
                P.grp("pe", ems, R=[wreg, ("xnef", 0, BIG)], W=[(pn, po, po + nw)])
                gr = ("gbuf", m * CP + no, m * CP + no + nw)
                P.grp("dve", [lambda E_, o=gbuf[:, m * CP + no:m * CP + no + nw],
                              i=pt[:, po:po + nw]:
                              E_.scalar_tensor_tensor(o, o, 7.0, i, op0=ALU.min, op1=ALU.mult)],
                      R=[(pn, po, po + nw), gr], W=[gr])
        if e + 1 < E:
            fetch_expert_transposes(e + 1)
        for m in range(DC):
            st_f.issue(1)
            wsl, wreg = st_f.slot(item); item += 1
            PSU = EV_PS if m % 2 == 0 else OD_PS
            for ni, (no, nw) in enumerate(NT):
                pt, pn, po = PSU[ni]
                ems = [lambda E_, o=pt[:, po:po + nw], l=wsl[:, k * 128:(k + 1) * 128],
                       r=gbuf[:, k * CP + no:k * CP + no + nw],
                       st=(k == 0), sp=(k == IC - 1):
                       E_.matmul(o, l, r, start=st, stop=sp) for k in range(IC)]
                P.grp("pe", ems, R=[wreg, ("gbuf", 0, BIG)], W=[(pn, po, po + nw)])
                P.grp("act", [lambda E_, o=hfm[:, m * CP + no:m * CP + no + nw],
                              i=pt[:, po:po + nw]: E_.activation(o, i, AF.Copy)],
                      R=[(pn, po, po + nw)],
                      W=[("hfm", m * CP + no, m * CP + no + nw)])
        sl = e % 2
        for g in range(G):
            hs = g % 2
            for m in range(DC):
                pt, pn = PT4[m % 4]
                off = ((m // 4) % 8) * 128
                P.grp("pe", [lambda E_, o=pt[:, off:off + 128],
                             i=hfm[:, m * CP + g * 128:m * CP + (g + 1) * 128]:
                             E_.transpose(o, i, cidbf[:, :])],
                      R=[("hfm", m * CP + g * 128, m * CP + (g + 1) * 128)],
                      W=[(pn, off, off + 128)])
                P.grp("act", [lambda E_, o=htm[:, hs * D + m * 128:hs * D + (m + 1) * 128],
                              i=pt[:, off:off + 128], w=wg_s[:, sl * G + g:sl * G + g + 1]:
                              E_.activation(o, i, AF.Copy, scale=w)],
                      R=[(pn, off, off + 128), ("wg_s", sl * G + g, sl * G + g + 1)],
                      W=[("htm", hs * D + m * 128, hs * D + (m + 1) * 128)])
            P.grp("pool", [lambda E_, i=htm[:, hs * D:(hs + 1) * D],
                           po=idxs[:, sl * G + g:sl * G + g + 1]:
                           E_.indirect_dma_start(doutf,
                                                 bass.IndirectOffsetOnAxis(ap=po, axis=1),
                                                 i, None, compute_op=ALU.add,
                                                 bounds_check=_breg(E_, T - 1), oob_is_err=False)],
                  R=[("htm", hs * D, (hs + 1) * D),
                     ("idxs", sl * G + g, sl * G + g + 1), ("out", 0, BIG)],
                  W=[("out", 0, BIG)], dma=True, dsem="dSC")
    es.close()
    P.barrier()

    P.barrier()

    global _LAST_P
    _LAST_P = P
    es_sem = ExitStack()
    sems = {s: es_sem.enter_context(nc.semaphore(s)) for s in sorted(P.semnames)}
    with nc.Block() as block:
        P.finalize(block, sems)
    es_sem.close()
    es0.close()
    return nc


def kernel(L0_attn_x, L0_attn_Wq, L0_attn_Wk, L0_attn_Wv, L0_attn_Wo,
           L0_ffn_Wr, L0_ffn_Wg, L0_ffn_Wu, L0_ffn_Wd):
    cfg = FULL_CFG
    nc = build_program(cfg)
    base = prep_inputs(cfg, np.zeros((1, 1)), L0_attn_Wq, L0_attn_Wk, L0_attn_Wv,
                       L0_attn_Wo, L0_ffn_Wr, L0_ffn_Wg, L0_ffn_Wu, L0_ffn_Wd)
    x = np.asarray(L0_attn_x, np.float32)
    in_maps = []
    for b in range(8):
        m = dict(base)
        m["x"] = np.ascontiguousarray(x[b])
        in_maps.append(m)
    res = run_bass_kernel_spmd(nc, in_maps, core_ids=list(range(8)))
    return np.stack([res.results[b]["out"] for b in range(8)], axis=0)


# version 24
# speedup vs baseline: 1.0972x; 1.0972x over previous
import numpy as np
import ml_dtypes
from contextlib import ExitStack
import concourse.bass as bass
import concourse.mybir as mybir
from concourse.bass_utils import run_bass_kernel_spmd

F32 = mybir.dt.float32
BF16 = mybir.dt.bfloat16
I32 = mybir.dt.int32
AF = mybir.ActivationFunctionType
ALU = mybir.AluOpType
AX = mybir.AxisListType

EXP_SCALE = float(1.0 / np.sqrt(128.0))
LN_EPS = 1e-5
NEG_BIG = -1.0e30

FULL_CFG = dict(D=2048, T=2048, H=16, E=8, I=2048)
_LAST_P = None

ENGS = ("pe", "act", "dve", "pool", "sp")
ESEM = {"pe": "sPE", "act": "sACT", "dve": "sDVE", "pool": "sPOOL", "sp": "sSP"}
DSEM = {"pool": "dPOOL", "sp": "dSP"}
ALL_SEMS = list(ESEM.values()) + list(DSEM.values())
BIG = 1 << 30


class Prog:

    def __init__(self):
        from collections import defaultdict
        self.q = {e: [] for e in ENGS}
        self.cnt = defaultdict(int)
        self.hw = {e: defaultdict(int) for e in ENGS}
        self.reg = {}
        self.semnames = set(ESEM.values())
        self.psum_gran = {}

    def _wait(self, eng, sem, val):
        if val > self.hw[eng][sem]:
            self.hw[eng][sem] = val
            self.q[eng].append(("w", sem, val))

    def _gran(self, t, lo, hi):
        g = self.psum_gran.get(t)
        if g:
            lo = (lo // g) * g
            hi = ((hi + g - 1) // g) * g
        return lo, hi

    def grp(self, eng, emitters, R=(), W=(), dma=False, dsem=None):
        R = [(t,) + self._gran(t, lo, hi) for (t, lo, hi) in R]
        W = [(t,) + self._gran(t, lo, hi) for (t, lo, hi) in W]
        waits = {}
        for (t, lo, hi) in R:
            rec = self.reg.get(t)
            if rec:
                for (l, h, s, v) in rec["w"]:
                    if l < hi and lo < h and v > waits.get(s, 0):
                        waits[s] = v
        for (t, lo, hi) in W:
            rec = self.reg.get(t)
            if rec:
                for (l, h, s, v) in rec["w"]:
                    if l < hi and lo < h and v > waits.get(s, 0):
                        waits[s] = v
                for (l, h, s, v) in rec["r"]:
                    if l < hi and lo < h and v > waits.get(s, 0):
                        waits[s] = v
        for s, v in waits.items():
            self._wait(eng, s, v)
        sem = (dsem or DSEM[eng]) if dma else ESEM[eng]
        self.semnames.add(sem)
        k = 16 if dma else 1
        self.cnt[sem] += k
        val = self.cnt[sem]
        self.q[eng].append(("e", list(emitters), sem, k))
        for (t, lo, hi) in R:
            self.reg.setdefault(t, {"w": [], "r": []})["r"].append((lo, hi, sem, val))
        for (t, lo, hi) in W:
            self.reg.setdefault(t, {"w": [], "r": []})["w"].append((lo, hi, sem, val))
        return (sem, val)

    def barrier(self, skip=None):
        snap = [(s, v) for s, v in self.cnt.items() if v]
        for e in ENGS:
            sk = (skip or {}).get(e, ())
            for s, v in snap:
                if s in sk:
                    continue
                self._wait(e, s, v)
        self.reg = {}

    def finalize(self, block, sems):
        secmap = {"pe": block.tensor, "act": block.scalar, "dve": block.vector,
                  "pool": block.gpsimd, "sp": block.sync}
        for e in ENGS:
            ops = self.q[e]

            def body(E, ops=ops):
                for item in ops:
                    if item[0] == "w":
                        E.wait_ge(sems[item[1]], item[2])
                    else:
                        _, emitters, sem, k = item
                        n = len(emitters)
                        for i, f in enumerate(emitters):
                            inst = f(E)
                            if i == n - 1:
                                inst.then_inc(sems[sem], k)
            secmap[e](body)


class Stream:

    def __init__(self, P, name, tensor, width, nslots, srcs, queue="pool"):
        self.P, self.name, self.tensor = P, name, tensor
        self.width, self.nslots, self.srcs = width, nslots, srcs
        self.queue, self.next = queue, 0

    def issue(self, n=1):
        for _ in range(n):
            if self.next >= len(self.srcs):
                return
            i = self.next
            self.next += 1
            s = i % self.nslots
            lo, hi = s * self.width, (s + 1) * self.width
            src = self.srcs[i]
            if len(src.shape) == 3:
                dst = self.tensor[:, lo:lo + src.shape[1] * src.shape[2]]
                dst = dst.rearrange("p (s c) -> p s c", s=src.shape[1])
            else:
                dst = self.tensor[:, lo:lo + src.shape[-1]]
            self.P.grp(self.queue,
                       [lambda E, d=dst, s_=src: E.dma_start(d, s_)],
                       W=[(self.name, lo, hi)], dma=True,
                       dsem="d_%s%d" % (self.name, s))

    def slot(self, i):
        s = i % self.nslots
        return self.tensor[:, s * self.width:(s + 1) * self.width], \
            (self.name, s * self.width, (s + 1) * self.width)



def _colchunks(W, KC, MC):
    return np.ascontiguousarray(
        W.reshape(KC, 128, MC, 128).transpose(2, 1, 0, 3).reshape(MC, 128, KC * 128))


def prep_inputs(cfg, x, Wq, Wk, Wv, Wo, Wr, Wg, Wu, Wd):
    D, T, H, E, I = cfg["D"], cfg["T"], cfg["H"], cfg["E"], cfg["I"]
    KC, DC, IC, NG = D // 128, D // 128, I // 128, H // 4
    f32 = lambda a: np.asarray(a, np.float32)
    m = {}
    m["wq"] = _colchunks(f32(Wq), KC, H)
    m["wk"] = _colchunks(f32(Wk), KC, H)
    m["wv"] = _colchunks(f32(Wv), KC, H)
    wo = f32(Wo).reshape(NG, 4, 128, DC, 128)
    m["wo"] = np.ascontiguousarray(wo.transpose(0, 3, 2, 1, 4).reshape(NG, DC, 128, 512))
    m["wr"] = np.ascontiguousarray(
        f32(Wr).reshape(KC, 128, E).transpose(1, 0, 2).reshape(128, KC * E))
    m["wg"] = np.stack([_colchunks(f32(Wg[e]), KC, IC) for e in range(E)])
    m["wu"] = np.stack([_colchunks(f32(Wu[e]), KC, IC) for e in range(E)])
    m["wd"] = np.stack([_colchunks(f32(Wd[e]), IC, DC) for e in range(E)])
    m["cid32"] = np.eye(128, dtype=np.float32)
    m["ceps"] = np.full((128, 1), 1e-5, dtype=np.float32)
    m["cidbf"] = np.eye(128).astype(ml_dtypes.bfloat16)
    lower = (np.arange(128)[:, None] > np.arange(128)[None, :]).astype(np.float64)
    m["cmask"] = np.concatenate(
        [np.ones((128, 384)), lower, np.zeros((128, 384))], axis=1).astype(ml_dtypes.bfloat16)
    ccol = np.zeros((128, 512), dtype=ml_dtypes.bfloat16)
    ccol[:, 511] = 1.0
    m["ccol"] = ccol
    m["conesb"] = np.ones((128, 16), dtype=ml_dtypes.bfloat16)
    m["cones32"] = np.ones((1, 128), dtype=np.float32)
    m["cones1b"] = np.ones((1, 128), dtype=ml_dtypes.bfloat16)
    TC = T // 128
    m["ltci"] = (np.arange(128)[:, None] <= np.arange(128)[None, :]).astype(ml_dtypes.bfloat16)
    sel = np.zeros((128, 128), dtype=np.float32)
    for jp in range(TC):
        for ep in range(E):
            for jj in range(TC):
                if jp < jj:
                    sel[jp * 8 + ep, jj * 8 + ep] = 1.0
    m["sel"] = sel
    m["ctok"] = (np.arange(TC)[None, :] * 128 + np.arange(128)[:, None]).astype(np.int32)
    CP = max(256, ((5 * T // 16 + 127) // 128) * 128)
    ecap = np.tile((np.arange(E) * CP).astype(np.float32), T // 128)
    m["ecap"] = np.broadcast_to(ecap, (128, T // 128 * E)).copy()
    m["x"] = np.ascontiguousarray(f32(x))
    return m



def build_program(cfg, phases=("ln", "router", "att", "ffn", "final")):
    D, T, H, E, I = cfg["D"], cfg["T"], cfg["H"], cfg["E"], cfg["I"]
    KC, DC, IC = D // 128, D // 128, I // 128
    TC, QC, NG = T // 128, T // 512, H // 4
    assert H % 4 == 0 and T % 512 == 0 and D == I and E <= 16

    nc = bass.Bass("TRN2", target_bir_lowering=False)

    dx = nc.dram_tensor("x", [T, D], F32, kind="ExternalInput").ap()
    dwq = nc.dram_tensor("wq", [H, 128, D], F32, kind="ExternalInput").ap()
    dwk = nc.dram_tensor("wk", [H, 128, D], F32, kind="ExternalInput").ap()
    dwv = nc.dram_tensor("wv", [H, 128, D], F32, kind="ExternalInput").ap()
    dwo = nc.dram_tensor("wo", [NG, DC, 128, 512], F32, kind="ExternalInput").ap()
    dwr = nc.dram_tensor("wr", [128, KC * E], F32, kind="ExternalInput").ap()
    dwg = nc.dram_tensor("wg", [E, IC, 128, D], F32, kind="ExternalInput").ap()
    dwu = nc.dram_tensor("wu", [E, IC, 128, D], F32, kind="ExternalInput").ap()
    dwd = nc.dram_tensor("wd", [E, DC, 128, I], F32, kind="ExternalInput").ap()
    dcid32 = nc.dram_tensor("cid32", [128, 128], F32, kind="ExternalInput").ap()
    dceps = nc.dram_tensor("ceps", [128, 1], F32, kind="ExternalInput").ap()
    dcidbf = nc.dram_tensor("cidbf", [128, 128], BF16, kind="ExternalInput").ap()
    dcmask = nc.dram_tensor("cmask", [128, 896], BF16, kind="ExternalInput").ap()
    dccol = nc.dram_tensor("ccol", [128, 512], BF16, kind="ExternalInput").ap()
    dconesb = nc.dram_tensor("conesb", [128, 16], BF16, kind="ExternalInput").ap()
    dcones32 = nc.dram_tensor("cones32", [1, 128], F32, kind="ExternalInput").ap()
    dcones1b = nc.dram_tensor("cones1b", [1, 128], BF16, kind="ExternalInput").ap()
    dltci = nc.dram_tensor("ltci", [128, 128], BF16, kind="ExternalInput").ap()
    dsel = nc.dram_tensor("sel", [128, 128], F32, kind="ExternalInput").ap()
    dctok = nc.dram_tensor("ctok", [128, TC], I32, kind="ExternalInput").ap()
    decap = nc.dram_tensor("ecap", [128, TC * 8], F32, kind="ExternalInput").ap()
    dout = nc.dram_tensor("out", [T, D], F32, kind="ExternalOutput").ap()
    wdram = nc.dram_tensor("wdram", [E, T], F32).ap()
    CP = max(256, ((5 * T // 16 + 127) // 128) * 128)
    G = CP // 128
    xnd = nc.dram_tensor("xnd", [T, D], BF16).ap()
    idxdA = nc.dram_tensor("idxdA", [1, E * CP], I32).ap()
    idxdB = nc.dram_tensor("idxdB", [1, E * CP], I32).ap()

    P = Prog()
    es0 = ExitStack()
    sb = lambda n, shp, dt, st: st.enter_context(nc.sbuf_tensor(n, shp, dt))

    def ps(n, shp, dt, st):
        h = st.enter_context(nc.psum_tensor(n, shp, dt))
        P.psum_gran[n] = 2048 // (4 if dt == F32 else 2)
        return h

    es_xn = ExitStack()
    xn = es_xn.enter_context(nc.sbuf_tensor("xn", [128, KC * T], BF16, side="right"))
    acc = sb("acc", [128, DC * T], BF16, es0)
    cid32 = sb("cid32s", [128, 128], F32, es0)
    ceps = sb("cepss", [128, 1], F32, es0)
    nc.const_aps.register(F32, LN_EPS, ceps[:, 0:1])
    cidbf = sb("cidbfs", [128, 128], BF16, es0)
    cmask = sb("cmasks", [128, 896], BF16, es0)
    ccol = sb("ccols", [128, 512], BF16, es0)
    conesb = sb("conesbs", [128, 16], BF16, es0)
    cones32 = sb("cones32s", [1, 128], F32, es0)
    cones1b = sb("cones1bs", [1, 128], BF16, es0)
    ltci = sb("ltcis", [128, 128], BF16, es0)
    sel = sb("sels", [128, 128], F32, es0)
    ctok = sb("ctoks", [128, TC], I32, es0)
    ecapw = sb("ecaps", [128, TC * 8], F32, es0)
    posi = sb("posi", [128, 2 * TC], I32, es0)
    ws = sb("ws", [128, 2 * D], BF16, es0)
    wos = sb("wos", [128, 2 * 1024], BF16, es0)

    bregs = {}

    def _breg(E_, val):
        r = bregs.get(val)
        if r is None:
            r = E_.alloc_register()
            E_.reg_mov(r, val)
            bregs[val] = r
        return r

    for nm, dst, src in [("cid32", cid32, dcid32), ("ceps", ceps, dceps),
                         ("cidbf", cidbf, dcidbf),
                         ("cmask", cmask, dcmask), ("ccol", ccol, dccol),
                         ("conesb", conesb, dconesb), ("cones32", cones32, dcones32),
                         ("cones1b", cones1b, dcones1b), ("ltci", ltci, dltci),
                         ("sel", sel, dsel), ("ctok", ctok, dctok),
                         ("ecapw", ecapw, decap)]:
        P.grp("sp", [lambda E, d=dst, s=src: E.dma_start(d[:, :], s[:, :])],
              W=[(nm, 0, BIG)], dma=True, dsem="dCONST")
    P.barrier()

    es = ExitStack()
    ln_on = "ln" in phases
    xin = sb("xin", [128, 4 * D], F32, es)
    xtmp = sb("xtmp", [128, 2 * D], F32, es)
    stats = sb("stats", [128, TC * 12], F32, es)
    sqbuf = sb("sqbuf", [128, 512], F32, es)
    wrb = sb("wrb", [128, KC * E], BF16, es)
    wsb = sb("wsb", [8, T], BF16, es)
    Lrt = sb("Lrt", [128, TC * 8], F32, es)
    Lm = sb("Lm", [128, TC * 8], F32, es)
    eq1 = sb("eq1", [128, TC * 8], F32, es)
    eq2 = sb("eq2", [128, TC * 8], F32, es)
    tkp = sb("tkp", [128, TC * 8], F32, es)
    wtm = sb("wtm", [128, TC * 8], F32, es)
    flf = sb("flf", [128, TC * 8], F32, es)
    flg = sb("flg", [128, TC * 8], F32, es)
    flb = sb("flb", [128, TC * 8], BF16, es)
    incl = sb("incl", [128, TC * 8], F32, es)
    poss = sb("poss", [128, TC * 8], F32, es)
    ctrc = sb("ctrc", [128, 1], F32, es)
    crow = sb("crow", [1, 128], F32, es)
    orow = sb("orow", [1, 128], F32, es)
    idfill = sb("idfill", [128, E * CP // 128], I32, es)
    pos12 = sb("pos12", [128, TC * 2], F32, es)
    NJE = TC * 8
    pR = ps("pR", [128, 512], F32, es)
    pR2 = ps("pR2", [128, 512], F32, es)
    pRt = ps("pRt", [8, 128], F32, es)
    es_tr = ExitStack()
    ptrA = ps("ptrA", [128, 1024], F32, es_tr)
    ptrB = ps("ptrB", [128, 1024], F32, es_tr)
    if "router" in phases:
        P.grp("pool", [lambda E: E.dma_start(wrb[:, :], dwr[:, :])],
              W=[("wrb", 0, KC * E)], dma=True, dsem="dWRB")
        P.grp("pool", [lambda E_: E_.memset(idfill[:, :], 2 * T)],
              W=[("idfill", 0, BIG)])
        P.grp("pool", [lambda E_: E_.dma_start(idxdA[0:1, :], idfill[:, :])],
              R=[("idfill", 0, BIG)], W=[("idxdA", 0, BIG)], dma=True, dsem="dIDFA")
        P.grp("pool", [lambda E_: E_.dma_start(idxdB[0:1, :], idfill[:, :])],
              R=[("idfill", 0, BIG)], W=[("idxdB", 0, BIG)], dma=True, dsem="dIDFB")

    def router_j(j):
        pr, prn = (pR, "pR") if j % 2 == 0 else (pR2, "pR2")
        ems = [lambda E, o=pr[:, 0:E], l=xn[:, k * T + j * 128:k * T + (j + 1) * 128],
               r=wrb[:, k * E:(k + 1) * E], st=(k == 0), sp=(k == KC - 1):
               E.matmul(o, l, r, start=st, stop=sp) for k in range(KC)]
        P.grp("pe", ems, R=[("xn", 0, KC * T), ("wrb", 0, KC * E)], W=[(prn, 0, E)])
        RL = ("Lrt", j * 8, j * 8 + E)
        P.grp("dve", [lambda E, o=Lrt[:, j * 8:j * 8 + E], i=pr[:, 0:E]: E.tensor_copy(o, i)],
              R=[(prn, 0, E)], W=[RL])
        K_ = lambda t, i: (t, j * 8 + i, j * 8 + i + 1)
        Le = Lrt[:, j * 8:j * 8 + E]
        P.grp("dve", [lambda E, o=tkp[:, j * 8:j * 8 + 1], i=Le: E.reduce_max(o, i, axis=AX.X)],
              R=[RL], W=[K_("tkp", 0)])
        P.grp("dve", [lambda E, o=eq1[:, j * 8:j * 8 + E], i=Le, m=tkp[:, j * 8:j * 8 + 1]:
                      E.tensor_scalar(o, i, m, None, op0=ALU.is_equal)],
              R=[RL, K_("tkp", 0)], W=[("eq1", j * 8, j * 8 + E)])
        P.grp("dve", [lambda E, o=Lm[:, j * 8:j * 8 + E], i=eq1[:, j * 8:j * 8 + E], l=Le:
                      E.scalar_tensor_tensor(o, i, NEG_BIG, l, op0=ALU.mult, op1=ALU.add)],
              R=[("eq1", j * 8, j * 8 + E), RL], W=[("Lm", j * 8, j * 8 + E)])
        P.grp("dve", [lambda E, o=tkp[:, j * 8 + 1:j * 8 + 2], i=Lm[:, j * 8:j * 8 + E]:
                      E.reduce_max(o, i, axis=AX.X)],
              R=[("Lm", j * 8, j * 8 + E)], W=[K_("tkp", 1)])
        P.grp("dve", [lambda E, o=eq2[:, j * 8:j * 8 + E], i=Lm[:, j * 8:j * 8 + E],
                      m=tkp[:, j * 8 + 1:j * 8 + 2]:
                      E.tensor_scalar(o, i, m, None, op0=ALU.is_equal)],
              R=[("Lm", j * 8, j * 8 + E), K_("tkp", 1)], W=[("eq2", j * 8, j * 8 + E)])
        P.grp("dve", [lambda E, o=tkp[:, j * 8 + 2:j * 8 + 3], a=tkp[:, j * 8 + 1:j * 8 + 2],
                      b=tkp[:, j * 8:j * 8 + 1]: E.tensor_tensor(o, a, b, op=ALU.subtract)],
              R=[K_("tkp", 0), K_("tkp", 1)], W=[K_("tkp", 2)])
        P.grp("act", [lambda E, o=tkp[:, j * 8 + 3:j * 8 + 4], i=tkp[:, j * 8 + 2:j * 8 + 3]:
                      E.activation(o, i, AF.Sigmoid)], R=[K_("tkp", 2)], W=[K_("tkp", 3)])
        P.grp("act", [lambda E, o=tkp[:, j * 8 + 4:j * 8 + 5], i=tkp[:, j * 8 + 2:j * 8 + 3]:
                      E.activation(o, i, AF.Sigmoid, scale=-1.0)], R=[K_("tkp", 2)], W=[K_("tkp", 4)])
        P.grp("dve", [lambda E, o=wtm[:, j * 8:j * 8 + E], i=eq1[:, j * 8:j * 8 + E],
                      p=tkp[:, j * 8 + 4:j * 8 + 5]:
                      E.tensor_scalar(o, i, p, None, op0=ALU.mult)],
              R=[("eq1", j * 8, j * 8 + E), K_("tkp", 4)], W=[("wtm", j * 8, j * 8 + E)])
        P.grp("dve", [lambda E, o=Lm[:, j * 8:j * 8 + E], i=eq2[:, j * 8:j * 8 + E],
                      p=tkp[:, j * 8 + 3:j * 8 + 4]:
                      E.tensor_scalar(o, i, p, None, op0=ALU.mult)],
              R=[("eq2", j * 8, j * 8 + E), K_("tkp", 3)], W=[("Lm", j * 8, j * 8 + E)])
        P.grp("dve", [lambda E, o=wtm[:, j * 8:j * 8 + E], b=Lm[:, j * 8:j * 8 + E]:
                      E.tensor_tensor(o, o, b, op=ALU.add)],
              R=[("wtm", j * 8, j * 8 + E), ("Lm", j * 8, j * 8 + E)],
              W=[("wtm", j * 8, j * 8 + E)])
        P.grp("pe", [lambda E, o=pRt[0:E, 0:128], i=wtm[:, j * 8:j * 8 + E]:
                     E.transpose(o, i, cid32[:, :])],
              R=[("wtm", j * 8, j * 8 + E)], W=[("pRt", 0, 128)])
        P.grp("dve", [lambda E, o=wsb[0:E, j * 128:(j + 1) * 128], i=pRt[0:E, 0:128]:
                      E.tensor_copy(o, i)],
              R=[("pRt", 0, 128)], W=[("wsb", j * 128, (j + 1) * 128)])
        P.grp("dve", [lambda E_, o=flf[:, j * 8:j * 8 + E], a=eq1[:, j * 8:j * 8 + E],
                      b=eq2[:, j * 8:j * 8 + E]: E_.tensor_tensor(o, a, b, op=ALU.add)],
              R=[("eq1", j * 8, j * 8 + E), ("eq2", j * 8, j * 8 + E)],
              W=[("flf", j * 8, j * 8 + E)])

    def x_load(j):
        slot = j % 4
        P.grp("sp", [lambda E, o=xin[:, slot * D:(slot + 1) * D],
                     i=dx[j * 128:(j + 1) * 128, :]: E.dma_start(o, i)],
              W=[("xin", slot * D, (slot + 1) * D)], dma=True, dsem="dXIN%d" % slot)

    def ln_a(j):
        slot = j % 4
        xi = xin[:, slot * D:(slot + 1) * D]
        Rxi = ("xin", slot * D, (slot + 1) * D)
        S = lambda i: stats[:, j * 12 + i:j * 12 + i + 1]
        Sr = lambda i: ("stats", j * 12 + i, j * 12 + i + 1)
        for q4 in range(D // 512):
            c_ = 0 if q4 == 0 else 7 + q4
            P.grp("act", [lambda E, i=xi[:, q4 * 512:(q4 + 1) * 512], a=S(c_):
                          E.activation(sqbuf[:, 0:512], i, AF.Square, accum_out=a)],
                  R=[Rxi], W=[("sqbuf", 0, 512), Sr(c_)])
        P.grp("dve", [lambda E, i=xi, o=S(1): E.reduce_sum(o, i, axis=AX.X)],
              R=[Rxi], W=[Sr(1)])

    def ln_b(j):
        S = lambda i: stats[:, j * 12 + i:j * 12 + i + 1]
        Sr = lambda i: ("stats", j * 12 + i, j * 12 + i + 1)
        if D > 512:
            P.grp("dve", [lambda E, o=S(0), a=S(8): E.tensor_tensor(o, o, a, op=ALU.add)],
                  R=[Sr(0), Sr(8)], W=[Sr(0)])
            P.grp("dve", [lambda E, o=S(9), a=S(10): E.tensor_tensor(o, o, a, op=ALU.add)],
                  R=[Sr(9), Sr(10)], W=[Sr(9)])
            P.grp("dve", [lambda E, o=S(0), a=S(9): E.tensor_tensor(o, o, a, op=ALU.add)],
                  R=[Sr(0), Sr(9)], W=[Sr(0)])
        P.grp("dve", [lambda E, o=S(2), i=S(1): E.tensor_scalar(o, i, 1.0 / D, None, op0=ALU.mult)],
              R=[Sr(1)], W=[Sr(2)])
        P.grp("dve", [lambda E, o=S(6), i=S(2): E.tensor_tensor(o, i, i, op=ALU.mult)],
              R=[Sr(2)], W=[Sr(6)])
        P.grp("dve", [lambda E, o=S(3), i=S(0), m2=S(6):
                      E.scalar_tensor_tensor(o, i, 1.0 / D, m2, op0=ALU.mult, op1=ALU.subtract)],
              R=[Sr(0), Sr(6)], W=[Sr(3)])
        P.grp("act", [lambda E, o=S(4), i=S(3): E.activation(o, i, AF.Ln, bias=LN_EPS)],
              R=[Sr(3)], W=[Sr(4)])
        P.grp("act", [lambda E, o=S(5), i=S(4): E.activation(o, i, AF.Exp, scale=-0.5)],
              R=[Sr(4)], W=[Sr(5)])

    def ln_c(j):
        slot = j % 4
        xi = xin[:, slot * D:(slot + 1) * D]
        Rxi = ("xin", slot * D, (slot + 1) * D)
        S = lambda i: stats[:, j * 12 + i:j * 12 + i + 1]
        Sr = lambda i: ("stats", j * 12 + i, j * 12 + i + 1)
        ts_ = (j % 2)
        xt = xtmp[:, ts_ * D:(ts_ + 1) * D]
        P.grp("dve", [lambda E, o=xt, i=xi, m=S(2), r=S(5):
                      E.tensor_scalar(o, i, m, r, op0=ALU.subtract, op1=ALU.mult)],
              R=[Rxi, Sr(2), Sr(5)], W=[("xtmp", ts_ * D, (ts_ + 1) * D)])
        P.grp("pool", [lambda E, o=xnd[j * 128:(j + 1) * 128, :], i=xt:
                       E.dma_start(o, i)],
              R=[("xtmp", ts_ * D, (ts_ + 1) * D)], W=[("xnd", j * D, (j + 1) * D)],
              dma=True, dsem="dXND%d" % ts_)
        for k in range(KC):
            pt, pn = (ptrA, "ptrA") if (k % 4) < 2 else (ptrB, "ptrB")
            off = (k % 2) * 512 + ((k // 4) % 4) * 128
            P.grp("pe", [lambda E, o=pt[:, off:off + 128],
                         i=xtmp[:, ts_ * D + k * 128:ts_ * D + (k + 1) * 128]:
                         E.transpose(o, i, cid32[:, :])],
                  R=[("xtmp", ts_ * D + k * 128, ts_ * D + (k + 1) * 128)],
                  W=[(pn, off, off + 128)])
            P.grp("dve", [lambda E, o=xn[:, k * T + j * 128:k * T + (j + 1) * 128],
                          i=pt[:, off:off + 128]: E.tensor_copy(o, i)],
                  R=[(pn, off, off + 128)],
                  W=[("xn", k * T + j * 128, k * T + (j + 1) * 128)])

    if ln_on:
        x_load(0)
    for jj in range(TC + 3 if ln_on else 0):
        if jj < TC:
            if jj + 1 < TC:
                x_load(jj + 1)
            ln_a(jj)
        if 1 <= jj and jj - 1 < TC:
            ln_b(jj - 1)
        if 2 <= jj and jj - 2 < TC:
            ln_c(jj - 2)
        if 3 <= jj and jj - 3 < TC and "router" in phases:
            router_j(jj - 3)
    es_tr.close()

    qkv_srcs = []
    for h in range(H):
        qkv_srcs += [dwq[h], dwk[h], dwv[h]]
    st_qkv = Stream(P, "ws", ws, D, 2, qkv_srcs)
    wo_srcs = []
    for hgi in range(NG):
        for m in range(0, DC, 2):
            wo_srcs.append(dwo[hgi, m:m + 2].rearrange("s p c -> p s c"))
    st_wo = Stream(P, "wos", wos, 1024, 2, wo_srcs)
    if "att" in phases:
        st_qkv.issue(1)
        st_wo.issue(1)

    if "router" in phases:
        P.grp("pool", [lambda E_, w=wsb[0:E, :]: E_.dma_start(wdram[:, :], w)],
              R=[("wsb", 0, T)], W=[("wdram", 0, E * T)], dma=True, dsem="dWDR")
    if "router" in phases:
        pX = ps("pX", [128, 128], F32, es)
        pY = ps("pY", [1, 128], F32, es)
        pZ = ps("pZ", [128, 128], F32, es)
        idxfA = idxdA[0:1, :].unsqueeze(2)
        idxfB = idxdB[0:1, :].unsqueeze(2)
        P.grp("dve", [lambda E_: E_.tensor_scalar(flg[:, 0:NJE], flf[:, 0:NJE], 0.0, None,
                                                  op0=ALU.is_gt)],
              R=[("flf", 0, NJE)], W=[("flg", 0, NJE)])
        P.grp("dve", [lambda E_: E_.tensor_copy(flb[:, 0:NJE], flg[:, 0:NJE])],
              R=[("flg", 0, NJE)], W=[("flb", 0, NJE)])
        P.grp("pe", [lambda E_, c=c: E_.matmul(pX[:, c:c + 1], ltci[:, :], flb[:, c:c + 1],
                                               start=True, stop=True) for c in range(NJE)],
              R=[("flb", 0, NJE), ("ltci", 0, 128)], W=[("pX", 0, 128)])
        P.grp("dve", [lambda E_: E_.tensor_copy(incl[:, 0:NJE], pX[:, 0:NJE])],
              R=[("pX", 0, 128)], W=[("incl", 0, NJE)])
        P.grp("dve", [lambda E_: E_.tensor_tensor(incl[:, 0:NJE], incl[:, 0:NJE],
                                                  flg[:, 0:NJE], op=ALU.subtract)],
              R=[("incl", 0, NJE), ("flg", 0, NJE)], W=[("incl", 0, NJE)])
        P.grp("dve", [lambda E_: E_.memset(crow[0:1, :], 0.0)], W=[("crow", 0, 128)])
        P.grp("pe", [lambda E_, c=c: E_.matmul(pY[0:1, c:c + 1], conesb[:, 0:1],
                                               flb[:, c:c + 1], start=True, stop=True)
                     for c in range(NJE)],
              R=[("flb", 0, NJE), ("conesb", 0, 1)], W=[("pY", 0, 128)])
        P.grp("act", [lambda E_: E_.copy(crow[0:1, 0:NJE], pY[0:1, 0:NJE])],
              R=[("pY", 0, 128)], W=[("crow", 0, NJE)])
        P.grp("pe", [lambda E_: E_.transpose(pZ[:, 0:1], crow[0:1, 0:128], cid32[0:1, 0:1])],
              R=[("crow", 0, 128)], W=[("pZ", 0, 128)])
        P.grp("dve", [lambda E_: E_.tensor_copy(ctrc[:, 0:1], pZ[:, 0:1])],
              R=[("pZ", 0, 128)], W=[("ctrc", 0, 1)])
        P.grp("pe", [lambda E_: E_.matmul(pY[0:1, 0:128], ctrc[:, 0:1], sel[:, :],
                                          start=True, stop=True)],
              R=[("ctrc", 0, 1), ("sel", 0, 128)], W=[("pY", 0, 128)])
        P.grp("act", [lambda E_: E_.copy(orow[0:1, :], pY[0:1, 0:128])],
              R=[("pY", 0, 128)], W=[("orow", 0, 128)])
        P.grp("pe", [lambda E_: E_.matmul(pZ[:, 0:128], cones32[0:1, 0:128],
                                          orow[0:1, 0:128], start=True, stop=True)],
              R=[("orow", 0, 128), ("cones32", 0, 128)], W=[("pZ", 0, 128)])
        P.grp("dve", [lambda E_: E_.tensor_tensor(poss[:, 0:NJE], incl[:, 0:NJE],
                                                  pZ[:, 0:NJE], op=ALU.add)],
              R=[("incl", 0, NJE), ("pZ", 0, 128)], W=[("poss", 0, NJE)])
        P.grp("dve", [lambda E_: E_.tensor_scalar(incl[:, 0:NJE], poss[:, 0:NJE],
                                                  float(CP - 1), 2.0e6, op0=ALU.is_gt,
                                                  op1=ALU.mult)],
              R=[("poss", 0, NJE)], W=[("incl", 0, NJE)])
        P.grp("dve", [lambda E_: E_.tensor_tensor(poss[:, 0:NJE], poss[:, 0:NJE],
                                                  incl[:, 0:NJE], op=ALU.add)],
              R=[("poss", 0, NJE), ("incl", 0, NJE)], W=[("poss", 0, NJE)])
        P.grp("dve", [lambda E_: E_.tensor_tensor(poss[:, 0:NJE], poss[:, 0:NJE],
                                                  ecapw[:, 0:NJE], op=ALU.add)],
              R=[("poss", 0, NJE), ("ecapw", 0, NJE)], W=[("poss", 0, NJE)])
        for j in range(TC):
            P.grp("dve", [lambda E_, o=incl[:, j * 8:j * 8 + E], a=poss[:, j * 8:j * 8 + E],
                          b=eq1[:, j * 8:j * 8 + E]: E_.tensor_tensor(o, a, b, op=ALU.mult)],
                  R=[("poss", j * 8, j * 8 + E), ("eq1", j * 8, j * 8 + E)],
                  W=[("incl", j * 8, j * 8 + E)])
            P.grp("dve", [lambda E_, o=pos12[:, 2 * j:2 * j + 1], i=incl[:, j * 8:j * 8 + E]:
                          E_.reduce_sum(o, i, axis=AX.X)],
                  R=[("incl", j * 8, j * 8 + E)], W=[("pos12", 2 * j, 2 * j + 1)])
            P.grp("dve", [lambda E_, o=incl[:, j * 8:j * 8 + E], a=poss[:, j * 8:j * 8 + E],
                          b=eq2[:, j * 8:j * 8 + E]: E_.tensor_tensor(o, a, b, op=ALU.mult)],
                  R=[("poss", j * 8, j * 8 + E), ("eq2", j * 8, j * 8 + E)],
                  W=[("incl", j * 8, j * 8 + E)])
            P.grp("dve", [lambda E_, o=pos12[:, 2 * j + 1:2 * j + 2], i=incl[:, j * 8:j * 8 + E]:
                          E_.reduce_sum(o, i, axis=AX.X)],
                  R=[("incl", j * 8, j * 8 + E)], W=[("pos12", 2 * j + 1, 2 * j + 2)])
        P.grp("dve", [lambda E_: E_.tensor_copy(posi[:, 0:2 * TC], pos12[:, 0:2 * TC])],
              R=[("pos12", 0, 2 * TC)], W=[("posi", 0, 2 * TC)])
        for j in range(TC):
            for s2, tgt, tn in ((0, idxfA, "idxdA"), (1, idxfB, "idxdB")):
                P.grp("pool",
                      [lambda E_, po=posi[:, 2 * j + s2:2 * j + s2 + 1], ti=ctok[:, j:j + 1],
                       t_=tgt: E_.indirect_dma_start(t_, bass.IndirectOffsetOnAxis(ap=po, axis=1),
                                                     ti, None,
                                                     bounds_check=_breg(E_, E * CP - 1),
                                                     oob_is_err=False)],
                      R=[("posi", 2 * j + s2, 2 * j + s2 + 1), ("ctok", j, j + 1)],
                      W=[(tn, 0, BIG)], dma=True, dsem="dIDX%d" % s2)
    es.close()
    _sk = {"dIDX0", "dIDX1"}
    P.barrier(skip={"pe": _sk, "act": _sk, "dve": _sk})

    es = ExitStack()
    qbf = sb("qbf", [128, T], BF16, es)
    kbf = sb("kbf", [128, T], BF16, es)
    v32 = sb("v32", [128, T], F32, es)
    vtm = sb("vtm", [128, T], BF16, es)
    expT = sb("expT", [128, TC * 512], BF16, es)
    attg = sb("attg", [128, 4 * T], BF16, es)
    srow = sb("srow", [1, 512], F32, es)
    srow2 = sb("srow2", [1, 512], F32, es)
    recrow = sb("recrow", [1, 1024], F32, es)
    recip = sb("recip", [128, 512], F32, es)
    pLa = ps("pLa", [128, 1024], F32, es)
    pLb = ps("pLb", [128, 1024], F32, es)
    pAV = ps("pAV", [128, 512], F32, es)
    pAV2 = ps("pAV2", [128, 512], F32, es)
    pS = ps("pS", [128, 512], F32, es)
    pS2 = ps("pS2", [128, 512], F32, es)

    QKV_PS = [(pLa, "pLa", 0), (pLa, "pLa", 512), (pLb, "pLb", 0), (pLb, "pLb", 512)]
    VT_PS = [(pAV, "pAV"), (pAV2, "pAV2"), (pS, "pS"), (pS2, "pS2")]
    att_pending = [None]

    for h in range(H if "att" in phases else 0):
        if att_pending[0] is not None:
            att_pending[0]()
            att_pending[0] = None
        for pi, (dst, dn, deng) in enumerate([(qbf, "qbf", "act"), (kbf, "kbf", "act"),
                                              (v32, "v32", "dve")]):
            st_qkv.issue(1)
            wsl, wreg = st_qkv.slot(3 * h + pi)
            for n in range(QC):
                pt, pn, po = QKV_PS[n]
                ems = [lambda E, o=pt[:, po:po + 512], l=wsl[:, k * 128:(k + 1) * 128],
                       r=xn[:, k * T + n * 512:k * T + n * 512 + 512],
                       st=(k == 0), sp=(k == KC - 1):
                       E.matmul(o, l, r, start=st, stop=sp) for k in range(KC)]
                P.grp("pe", ems, R=[wreg, ("xn", 0, KC * T)], W=[(pn, po, po + 512)])
                if deng == "act":
                    P.grp("act", [lambda E, o=dst[:, n * 512:(n + 1) * 512],
                                  i=pt[:, po:po + 512]: E.activation(o, i, AF.Copy)],
                          R=[(pn, po, po + 512)], W=[(dn, n * 512, (n + 1) * 512)])
                else:
                    P.grp("dve", [lambda E, o=dst[:, n * 512:(n + 1) * 512],
                                  i=pt[:, po:po + 512]: E.tensor_copy(o, i)],
                          R=[(pn, po, po + 512)], W=[(dn, n * 512, (n + 1) * 512)])
        for r in range(TC):
            pt, pn = VT_PS[r % 4]
            off = 128 * ((r // 4) % 4)
            P.grp("pe", [lambda E, o=pt[:, off:off + 128].bitcast(F32) if False else pt[:, off:off + 128],
                         i=v32[:, r * 128:(r + 1) * 128]: E.transpose(o, i, cid32[:, :])],
                  R=[("v32", r * 128, (r + 1) * 128)], W=[(pn, off, off + 128)])
            P.grp("dve", [lambda E, o=vtm[:, r * 128:(r + 1) * 128], i=pt[:, off:off + 128]:
                          E.tensor_copy(o, i)],
                  R=[(pn, off, off + 128)], W=[("vtm", r * 128, (r + 1) * 128)])
        for c in range(QC):
            real = list(range(4 * c, TC))
            cmcol = list(range(0, 4 * c)) if c == QC - 1 else []
            pi = 0
            idx = 0
            while idx < len(real):
                take = 2 if idx + 1 < len(real) else 1
                rs = real[idx:idx + take]
                pt, pn = (pLa, "pLa") if pi % 2 == 0 else (pLb, "pLb")
                ems = [lambda E, o=pt[:, u * 512:(u + 1) * 512],
                       l=kbf[:, r * 128:(r + 1) * 128],
                       r_=qbf[:, c * 512:(c + 1) * 512]:
                       E.matmul(o, l, r_, start=True, stop=True)
                       for u, r in enumerate(rs)]
                P.grp("pe", ems,
                      R=[("kbf", 0, T), ("qbf", c * 512, (c + 1) * 512)],
                      W=[(pn, 0, take * 512)])
                P.grp("act", [lambda E, o=expT[:, rs[0] * 512:(rs[0] + take) * 512],
                              i=pt[:, 0:take * 512]:
                              E.activation(o, i, AF.Exp, scale=EXP_SCALE)],
                      R=[(pn, 0, take * 512)],
                      W=[("expT", rs[0] * 512, (rs[0] + take) * 512)])
                pi += 1
                idx += take
            for r in range(4 * c, min(4 * c + 4, TC)):
                dlt = r - 4 * c
                P.grp("dve", [lambda E, o=expT[:, r * 512:(r + 1) * 512],
                              m=cmask[:, 384 - 128 * dlt:384 - 128 * dlt + 512]:
                              E.tensor_tensor(o, o, m, op=ALU.mult)],
                      R=[("expT", r * 512, (r + 1) * 512), ("cmask", 0, 896)],
                      W=[("expT", r * 512, (r + 1) * 512)])
            if c == QC - 1:
                for r in real:
                    P.grp("pool", [lambda E, o=expT[:, r * 512 + 511:r * 512 + 512]:
                                   E.memset(o, 1.0)],
                          W=[("expT", r * 512 + 511, r * 512 + 512)])
            pSx, pSn = (pS, "pS") if c % 2 == 0 else (pS2, "pS2")
            pAx, pAn = (pAV, "pAV") if c % 2 == 0 else (pAV2, "pAV2")
            cp_ = (c % 2) * 512
            rhs_list = [(expT[:, r * 512:(r + 1) * 512], ("expT", r * 512, (r + 1) * 512))
                        for r in real]
            rhs_list += [(ccol[:, 0:512], ("ccol", 0, 512)) for _ in cmcol]
            ems = [lambda E, o=pSx[0:1, 0:512], r_=rhs[0], st=(i == 0),
                   sp=(i == len(rhs_list) - 1):
                   E.matmul(o, conesb[:, 0:1], r_, start=st, stop=sp)
                   for i, rhs in enumerate(rhs_list)]
            P.grp("pe", ems, R=[rr[1] for rr in rhs_list] + [("conesb", 0, 1)],
                  W=[(pSn, 0, 512)])
            P.grp("act", [lambda E, o=srow[0:1, 0:512], i=pSx[0:1, 0:512]:
                          E.copy(o, i)],
                  R=[(pSn, 0, 512)], W=[("srow", 0, 512)])
            P.grp("act", [lambda E, o=srow2[0:1, 0:512], i=srow[0:1, 0:512]:
                          E.activation(o, i, AF.Ln)],
                  R=[("srow", 0, 512)], W=[("srow2", 0, 512)])
            P.grp("act", [lambda E, o=recrow[0:1, cp_:cp_ + 512], i=srow2[0:1, 0:512]:
                          E.activation(o, i, AF.Exp, scale=-1.0)],
                  R=[("srow2", 0, 512)], W=[("recrow", cp_, cp_ + 512)])
            vt_list = [vtm[:, r * 128:(r + 1) * 128] for r in real] + \
                      [vtm[:, r * 128:(r + 1) * 128] for r in cmcol]
            ems = [lambda E, o=pAx[:, 0:512], l=vt_list[i], r_=rhs_list[i][0],
                   st=(i == 0), sp=(i == len(rhs_list) - 1):
                   E.matmul(o, l, r_, start=st, stop=sp)
                   for i in range(len(rhs_list))]
            P.grp("pe", ems, R=[rr[1] for rr in rhs_list] + [("vtm", 0, T)],
                  W=[(pAn, 0, 512)])
            if att_pending[0] is not None:
                att_pending[0]()
            g_in = h % 4

            def epilogue(pSx=pSx, pSn=pSn, pAx=pAx, pAn=pAn, cp_=cp_, g_in=g_in, c=c):
                P.grp("pe", [lambda E, o=pSx[:, 0:512], r_=recrow[0:1, cp_:cp_ + 512]:
                             E.matmul(o, cones32[0:1, 0:128], r_, start=True, stop=True)],
                      R=[("recrow", cp_, cp_ + 512), ("cones32", 0, 128)], W=[(pSn, 0, 512)])
                P.grp("dve", [lambda E, o=recip[:, 0:512], i=pSx[:, 0:512]:
                              E.tensor_copy(o, i)],
                      R=[(pSn, 0, 512)], W=[("recip", 0, 512)])
                P.grp("dve", [lambda E, o=attg[:, g_in * T + c * 512:g_in * T + (c + 1) * 512],
                              a=pAx[:, 0:512], b=recip[:, 0:512]:
                              E.tensor_tensor(o, a, b, op=ALU.mult)],
                      R=[(pAn, 0, 512), ("recip", 0, 512)],
                      W=[("attg", g_in * T + c * 512, g_in * T + (c + 1) * 512)])
            att_pending[0] = epilogue
        if h % 4 == 3:
            if att_pending[0] is not None:
                att_pending[0]()
                att_pending[0] = None
            HGI = h // 4
            for m in range(DC):
                if m % 2 == 0:
                    st_wo.issue(1)
                wsl, wreg = st_wo.slot(HGI * (DC // 2) + m // 2)
                mo = (m % 2) * 512
                for n in range(QC):
                    pt, pn, po = QKV_PS[n] if m % 2 == 0 else \
                        [(pAV, "pAV", 0), (pAV2, "pAV2", 0), (pS, "pS", 0), (pS2, "pS2", 0)][n]
                    ems = [lambda E, o=pt[:, po:po + 512],
                           l=wsl[:, mo + ks * 128:mo + (ks + 1) * 128],
                           r=attg[:, ks * T + n * 512:ks * T + n * 512 + 512],
                           st=(ks == 0), sp=(ks == 3):
                           E.matmul(o, l, r, start=st, stop=sp) for ks in range(4)]
                    P.grp("pe", ems, R=[wreg, ("attg", 0, 4 * T)], W=[(pn, po, po + 512)])
                    arange = ("acc", m * T + n * 512, m * T + (n + 1) * 512)
                    if HGI == 0:
                        P.grp("dve", [lambda E, o=acc[:, m * T + n * 512:m * T + (n + 1) * 512],
                                      i=pt[:, po:po + 512]: E.tensor_copy(o, i)],
                              R=[(pn, po, po + 512)], W=[arange])
                    else:
                        P.grp("dve", [lambda E, o=acc[:, m * T + n * 512:m * T + (n + 1) * 512],
                                      i=pt[:, po:po + 512]:
                                      E.tensor_tensor(o, i, o, op=ALU.add)],
                              R=[(pn, po, po + 512), arange], W=[arange])
    es.close()
    P.barrier()

    es_xn.close()

    es = ExitStack()
    xin2 = sb("xin2", [128, D], F32, es)
    osb = sb("osb", [128, D], F32, es)
    fs = sb("fs", [128, 4 * D], BF16, es)
    pA = ps("pA", [128, 1024], F32, es)
    pB = ps("pB", [128, 1024], F32, es)
    pTa2 = ps("pTa2", [128, 1024], BF16, es)
    pTb2 = ps("pTb2", [128, 1024], BF16, es)
    pTc2 = ps("pTc2", [128, 1024], BF16, es)
    pTd2 = ps("pTd2", [128, 1024], BF16, es)
    PT4 = [(pTa2, "pTa2"), (pTb2, "pTb2"), (pTc2, "pTc2"), (pTd2, "pTd2")]
    xnef = sb("xnef", [128, KC * CP], BF16, es)
    gbuf = sb("gbuf", [128, IC * CP], BF16, es)
    hfm = sb("hfm", [128, DC * CP], BF16, es)
    XGS = min(G, 4)
    xgat = sb("xgat", [128, XGS * D], BF16, es)
    htm = sb("htm", [128, 2 * D], F32, es)
    idxs = sb("idxs", [128, 2 * G], I32, es)
    idxab = sb("idxab", [128, 4 * G], I32, es)
    wg_s = sb("wg_s", [128, 2 * G], F32, es)
    xndf = xnd.unsqueeze(0)
    doutf = dout.unsqueeze(0)
    wdf = wdram.rearrange("e t -> (e t)").unsqueeze(0).unsqueeze(2)

    def residual_load(j):
        P.grp("sp", [lambda E_, o=xin2[:, :],
                     i=dx[j * 128:(j + 1) * 128, :]: E_.dma_start(o, i)],
              W=[("xin2", 0, D)], dma=True, dsem="dXIN0")

    def residual_comp(j):
        for m in range(DC):
            pt, pn = PT4[m % 4]
            off = ((m // 4) % 8) * 128
            P.grp("pe", [lambda E_, o=pt[:, off:off + 128],
                         i=acc[:, m * T + j * 128:m * T + (j + 1) * 128]:
                         E_.transpose(o, i, cidbf[:, :])],
                  R=[("acc", m * T + j * 128, m * T + (j + 1) * 128)],
                  W=[(pn, off, off + 128)])
            P.grp("dve", [lambda E_, o=osb[:, m * 128:(m + 1) * 128],
                          a=pt[:, off:off + 128],
                          b=xin2[:, m * 128:(m + 1) * 128]:
                          E_.tensor_tensor(o, a, b, op=ALU.add)],
                  R=[(pn, off, off + 128), ("xin2", m * 128, (m + 1) * 128)],
                  W=[("osb", m * 128, (m + 1) * 128)])
        P.grp("sp", [lambda E_, o=dout[j * 128:(j + 1) * 128, :], i=osb[:, :]:
                     E_.dma_start(o, i)],
              R=[("osb", 0, D)], W=[("out", j * D, (j + 1) * D)],
              dma=True, dsem="dOUT0")

    if "final" in phases and "ffn" not in phases:
        for j in range(TC):
            residual_load(j)
            residual_comp(j)

    ffn_srcs = []
    for e in range(E):
        for m in range(IC):
            ffn_srcs.append(dwg[e, m])
        for m in range(IC):
            ffn_srcs.append(dwu[e, m])
        for m in range(DC):
            ffn_srcs.append(dwd[e, m])
    st_f = Stream(P, "fs", fs, D, 4, ffn_srcs)
    item = 0

    NT = [(i * 512, min(512, CP - i * 512)) for i in range((CP + 511) // 512)]
    EV_PS = [(pA, "pA", 0), (pA, "pA", 512)]
    OD_PS = [(pB, "pB", 0), (pB, "pB", 512)]

    def fetch_expert_dmas(e):
        sl = e % 2
        P.grp("pool", [lambda E_, o=idxab[:, sl * 2 * G:sl * 2 * G + G],
                       i=idxdA[0:1, e * CP:(e + 1) * CP].rearrange("a (g p) -> (a p) g", p=128):
                       E_.dma_start(o, i, allow_slow_non_contiguous=True)],
              R=[("idxdA", 0, BIG)], W=[("idxab", sl * 2 * G, sl * 2 * G + G)],
              dma=True, dsem="dIXL%d" % sl)
        P.grp("pool", [lambda E_, o=idxab[:, sl * 2 * G + G:sl * 2 * G + 2 * G],
                       i=idxdB[0:1, e * CP:(e + 1) * CP].rearrange("a (g p) -> (a p) g", p=128):
                       E_.dma_start(o, i, allow_slow_non_contiguous=True)],
              R=[("idxdB", 0, BIG)], W=[("idxab", sl * 2 * G + G, sl * 2 * G + 2 * G)],
              dma=True, dsem="dIXM%d" % sl)
        P.grp("dve", [lambda E_, o=idxs[:, sl * G:(sl + 1) * G],
                      a=idxab[:, sl * 2 * G:sl * 2 * G + G],
                      b=idxab[:, sl * 2 * G + G:sl * 2 * G + 2 * G]:
                      E_.tensor_tensor(o, a, b, op=ALU.min)],
              R=[("idxab", sl * 2 * G, sl * 2 * G + 2 * G)],
              W=[("idxs", sl * G, (sl + 1) * G)])
        for g in range(G):
            P.grp("pool", [lambda E_, o=wg_s[:, sl * G + g:sl * G + g + 1],
                           po=idxs[:, sl * G + g:sl * G + g + 1], eo=e * T:
                           E_.indirect_dma_start(o, None, wdf,
                                                 bass.IndirectOffsetOnAxis(ap=po, axis=1),
                                                 element_offset=eo,
                                                 bounds_check=_breg(E_, T - 1), oob_is_err=False)],
                  R=[("idxs", sl * G + g, sl * G + g + 1), ("wdram", e * T, (e + 1) * T)],
                  W=[("wg_s", sl * G + g, sl * G + g + 1)], dma=True,
                  dsem="dWG%d_%d" % (sl, g))
            if g < XGS:
                xn_gather(e, g)

    def xn_gather(e, g):
        sl = e % 2
        P.grp("pool", [lambda E_, o=xgat[:, (g % XGS) * D:(g % XGS + 1) * D],
                       po=idxs[:, sl * G + g:sl * G + g + 1]:
                       E_.indirect_dma_start(o, None, xndf,
                                             bass.IndirectOffsetOnAxis(ap=po, axis=1),
                                             bounds_check=_breg(E_, T - 1), oob_is_err=False)],
              R=[("idxs", sl * G + g, sl * G + g + 1), ("xnd", 0, BIG)],
              W=[("xgat", (g % XGS) * D, (g % XGS + 1) * D)], dma=True,
              dsem="dXG%d" % (g % XGS))

    def fetch_expert_transpose_group(e, g):
        sl = e % 2
        if True:
            for k in range(KC):
                pt, pn = PT4[k % 4]
                off = ((k // 4) % 8) * 128
                P.grp("pe", [lambda E_, o=pt[:, off:off + 128],
                             i=xgat[:, (g % XGS) * D + k * 128:(g % XGS) * D + (k + 1) * 128]:
                             E_.transpose(o, i, cidbf[:, :])],
                      R=[("xgat", (g % XGS) * D + k * 128, (g % XGS) * D + (k + 1) * 128)],
                      W=[(pn, off, off + 128)])
                P.grp("dve", [lambda E_, o=xnef[:, k * CP + g * 128:k * CP + (g + 1) * 128],
                              i=pt[:, off:off + 128]: E_.tensor_copy(o, i)],
                      R=[(pn, off, off + 128)],
                      W=[("xnef", k * CP + g * 128, k * CP + (g + 1) * 128)])
            if g + XGS < G:
                xn_gather(e, g + XGS)

    def fetch_expert_transposes(e):
        for g in range(G):
            fetch_expert_transpose_group(e, g)

    def h_group(e, g):
        sl = e % 2
        hs = g % 2
        for m in range(DC):
            pt, pn = PT4[m % 4]
            off = ((m // 4) % 8) * 128
            P.grp("pe", [lambda E_, o=pt[:, off:off + 128],
                         i=hfm[:, m * CP + g * 128:m * CP + (g + 1) * 128]:
                         E_.transpose(o, i, cidbf[:, :])],
                  R=[("hfm", m * CP + g * 128, m * CP + (g + 1) * 128)],
                  W=[(pn, off, off + 128)])
            P.grp("act", [lambda E_, o=htm[:, hs * D + m * 128:hs * D + (m + 1) * 128],
                          i=pt[:, off:off + 128], w=wg_s[:, sl * G + g:sl * G + g + 1]:
                          E_.activation(o, i, AF.Copy, scale=w)],
                  R=[(pn, off, off + 128), ("wg_s", sl * G + g, sl * G + g + 1)],
                  W=[("htm", hs * D + m * 128, hs * D + (m + 1) * 128)])
        P.grp("pool", [lambda E_, i=htm[:, hs * D:(hs + 1) * D],
                       po=idxs[:, sl * G + g:sl * G + g + 1]:
                       E_.indirect_dma_start(doutf,
                                             bass.IndirectOffsetOnAxis(ap=po, axis=1),
                                             i, None, compute_op=ALU.add,
                                             bounds_check=_breg(E_, T - 1), oob_is_err=False)],
              R=[("htm", hs * D, (hs + 1) * D),
                 ("idxs", sl * G + g, sl * G + g + 1), ("out", 0, BIG)],
              W=[("out", 0, BIG)], dma=True, dsem="dSC")

    for e in range(E if "ffn" in phases else 0):
        if e == 0:
            fetch_expert_dmas(0)
            st_f.issue(3)
            if "final" in phases:
                residual_load(0)
            fetch_expert_transposes(0)
        for m in range(IC):
            st_f.issue(1)
            wsl, wreg = st_f.slot(item); item += 1
            PSU = EV_PS if m % 2 == 0 else OD_PS
            for ni, (no, nw) in enumerate(NT):
                pt, pn, po = PSU[ni]
                ems = [lambda E_, o=pt[:, po:po + nw], l=wsl[:, k * 128:(k + 1) * 128],
                       r=xnef[:, k * CP + no:k * CP + no + nw],
                       st=(k == 0), sp=(k == KC - 1):
                       E_.matmul(o, l, r, start=st, stop=sp) for k in range(KC)]
                P.grp("pe", ems, R=[wreg, ("xnef", 0, BIG)], W=[(pn, po, po + nw)])
                gr = ("gbuf", m * CP + no, m * CP + no + nw)
                P.grp("act", [lambda E_, o=gbuf[:, m * CP + no:m * CP + no + nw],
                              i=pt[:, po:po + nw]: E_.activation(o, i, AF.Silu)],
                      R=[(pn, po, po + nw)], W=[gr])
            if e == 0 and "final" in phases and m < TC:
                residual_comp(m)
                if m + 1 < TC:
                    residual_load(m + 1)
            if e >= 1 and m % 3 == 2 and (m - 2) // 3 < G:
                h_group(e - 1, (m - 2) // 3)
        if e >= 1:
            for g in range((IC + 1) // 3, G):
                h_group(e - 1, g)
        if e + 1 < E:
            fetch_expert_dmas(e + 1)
        for m in range(IC):
            st_f.issue(1)
            wsl, wreg = st_f.slot(item); item += 1
            PSU = EV_PS if m % 2 == 0 else OD_PS
            for ni, (no, nw) in enumerate(NT):
                pt, pn, po = PSU[ni]
                ems = [lambda E_, o=pt[:, po:po + nw], l=wsl[:, k * 128:(k + 1) * 128],
                       r=xnef[:, k * CP + no:k * CP + no + nw],
                       st=(k == 0), sp=(k == KC - 1):
                       E_.matmul(o, l, r, start=st, stop=sp) for k in range(KC)]
                P.grp("pe", ems, R=[wreg, ("xnef", 0, BIG)], W=[(pn, po, po + nw)])
                gr = ("gbuf", m * CP + no, m * CP + no + nw)
                P.grp("dve", [lambda E_, o=gbuf[:, m * CP + no:m * CP + no + nw],
                              i=pt[:, po:po + nw]:
                              E_.scalar_tensor_tensor(o, o, 7.0, i, op0=ALU.min, op1=ALU.mult)],
                      R=[(pn, po, po + nw), gr], W=[gr])
        for m in range(DC):
            if e + 1 < E and m % 3 == 1 and (m - 1) // 3 < G:
                fetch_expert_transpose_group(e + 1, (m - 1) // 3)
            st_f.issue(1)
            wsl, wreg = st_f.slot(item); item += 1
            PSU = EV_PS if m % 2 == 0 else OD_PS
            for ni, (no, nw) in enumerate(NT):
                pt, pn, po = PSU[ni]
                ems = [lambda E_, o=pt[:, po:po + nw], l=wsl[:, k * 128:(k + 1) * 128],
                       r=gbuf[:, k * CP + no:k * CP + no + nw],
                       st=(k == 0), sp=(k == IC - 1):
                       E_.matmul(o, l, r, start=st, stop=sp) for k in range(IC)]
                P.grp("pe", ems, R=[wreg, ("gbuf", 0, BIG)], W=[(pn, po, po + nw)])
                P.grp("act", [lambda E_, o=hfm[:, m * CP + no:m * CP + no + nw],
                              i=pt[:, po:po + nw]: E_.activation(o, i, AF.Copy)],
                      R=[(pn, po, po + nw)],
                      W=[("hfm", m * CP + no, m * CP + no + nw)])
        if e + 1 < E:
            for g in range((DC + 1) // 3, G):
                fetch_expert_transpose_group(e + 1, g)
        if e == E - 1:
            for g in range(G):
                h_group(e, g)
    es.close()
    P.barrier()

    P.barrier()

    global _LAST_P
    _LAST_P = P
    es_sem = ExitStack()
    sems = {s: es_sem.enter_context(nc.semaphore(s)) for s in sorted(P.semnames)}
    with nc.Block() as block:
        P.finalize(block, sems)
    es_sem.close()
    es0.close()
    return nc


def kernel(L0_attn_x, L0_attn_Wq, L0_attn_Wk, L0_attn_Wv, L0_attn_Wo,
           L0_ffn_Wr, L0_ffn_Wg, L0_ffn_Wu, L0_ffn_Wd):
    cfg = FULL_CFG
    nc = build_program(cfg)
    base = prep_inputs(cfg, np.zeros((1, 1)), L0_attn_Wq, L0_attn_Wk, L0_attn_Wv,
                       L0_attn_Wo, L0_ffn_Wr, L0_ffn_Wg, L0_ffn_Wu, L0_ffn_Wd)
    x = np.asarray(L0_attn_x, np.float32)
    in_maps = []
    for b in range(8):
        m = dict(base)
        m["x"] = np.ascontiguousarray(x[b])
        in_maps.append(m)
    res = run_bass_kernel_spmd(nc, in_maps, core_ids=list(range(8)))
    return np.stack([res.results[b]["out"] for b in range(8)], axis=0)
